# version 19
# baseline (speedup 1.0000x reference)
import numpy as np
import ml_dtypes
from contextlib import ExitStack

import concourse.bass as bass
import concourse.mybir as mybir
from concourse.bass_utils import run_bass_kernel_spmd

F32 = mybir.dt.float32
BF16 = mybir.dt.bfloat16
F8 = mybir.dt.float8e4
ALU = mybir.AluOpType
AF = mybir.ActivationFunctionType
AX = mybir.AxisListType

B, T, H = 4, 4096, 2048
NCORE = 8
CT = T // NCORE
G = H // 128
JH = 8192
NJ = JH // 128

C1 = float(np.float32(ml_dtypes.bfloat16(0.044715)))
C2 = float(np.float32(ml_dtypes.bfloat16(np.sqrt(2.0 / np.pi))))
NEG_LN2 = -0.6931471805599453

ENGS = ("sync", "gpsimd", "vector", "scalar", "tensor")


def build_program():
    nc = bass.Bass("TRN2", target_bir_lowering=False)

    x_d = nc.dram_tensor("x", [B, CT + 1, H], F32, kind="ExternalInput")
    wk_d = nc.dram_tensor("wk", [G, 128, G, 128], F32, kind="ExternalInput")
    wv_d = nc.dram_tensor("wv", [G, 128, G, 128], F32, kind="ExternalInput")
    wr_d = nc.dram_tensor("wr", [G, 128, G, 128], F32, kind="ExternalInput")
    wo_d = nc.dram_tensor("wo", [G, 4, 128, 512], F32, kind="ExternalInput")
    wg0_d = nc.dram_tensor("wg0", [NJ, 128, G, 128], F32, kind="ExternalInput")
    wg1_d = nc.dram_tensor("wg1", [NJ, 128, G, 128], F32, kind="ExternalInput")
    wl_d = nc.dram_tensor("wl", [JH, H], F32, kind="ExternalInput")
    tmk_d = nc.dram_tensor("tmk", [128, G], F32, kind="ExternalInput")
    tmv_d = nc.dram_tensor("tmv", [128, G], F32, kind="ExternalInput")
    tmr_d = nc.dram_tensor("tmr", [128, G], F32, kind="ExternalInput")
    td_d = nc.dram_tensor("td", [128, G], F32, kind="ExternalInput")
    tf_d = nc.dram_tensor("tf", [128, G], F32, kind="ExternalInput")
    ecnt_d = nc.dram_tensor("ecnt", [128, 8], F32, kind="ExternalInput")
    ident_d = nc.dram_tensor("ident", [128, 128], F32, kind="ExternalInput")
    out_d = nc.dram_tensor("out", [B, CT, H], F32, kind="ExternalOutput")

    wkb_d = nc.dram_tensor("wkb", [G, 128, G, 128], BF16)
    wvb_d = nc.dram_tensor("wvb", [G, 128, G, 128], BF16)
    wrb_d = nc.dram_tensor("wrb", [G, 128, G, 128], BF16)
    wob_d = nc.dram_tensor("wob", [G, 4, 128, 512], BF16)
    wg0b_d = nc.dram_tensor("wg0b", [NJ, 128, G, 128], F8)
    wg1b_d = nc.dram_tensor("wg1b", [NJ, 128, G, 128], F8)
    wlb_d = nc.dram_tensor("wlb", [JH, H], F8)
    attn_d = nc.dram_tensor("attn_sc", [B, CT, H], BF16)
    act_d = nc.dram_tensor("act_sc", [B, JH, CT], F8)
    agin = [nc.dram_tensor("agin%d" % b, [128, 32], F32) for b in range(B)]
    agout = [nc.dram_tensor("agout%d" % b, [1024, 32], F32) for b in range(B)]

    steps = {k: [] for k in ENGS}
    cnt = dict(lmisc=0, lx0=0, lx1=0, lact=0, lact1=0, lf0=0, lf1=0,
               so0=0, so1=0, gw0=0, gw1=0, gwl=0, gmisc=0,
               gs0=0, gs1=0, sa=0, sa1=0, v=0, s=0, t=0, cc=0, g=0,
               gck=0, gcv=0, gcr=0, gco=0, gc0=0, gc1=0, gcl=0)
    sems = {}

    def emit(eng, f):
        steps[eng].append(f)

    def W(eng, sem, val):
        if val <= 0:
            return
        emit(eng, lambda e, sname=sem, v=val: e.wait_ge(sems[sname], v))

    def dma(eng, sem, out_ap, in_ap):
        cnt[sem] += 16
        emit(eng, lambda e, o=out_ap, i=in_ap, sname=sem:
             e.dma_start(out=o, in_=i).then_inc(sems[sname], 16))
        return cnt[sem]

    _OWN = {"vector": "v", "scalar": "s", "gpsimd": "g"}

    def op(eng, sem, bld):
        own = _OWN.get(eng)
        if own is not None:
            assert sem == own
            W(eng, own, cnt[own])
        cnt[sem] += 1
        emit(eng, lambda e, bd=bld, sname=sem: bd(e).then_inc(sems[sname], 1))
        return cnt[sem]

    def opn(eng, bld):
        own = _OWN.get(eng)
        if own is not None:
            return op(eng, own, bld)
        emit(eng, lambda e, bd=bld: bd(e))

    def mark(eng, sem):
        cnt[sem] += 1
        emit(eng, lambda e, sname=sem: e.nop().then_inc(sems[sname], 1))
        return cnt[sem]

    ctx = ExitStack()
    with ctx:
        for name in ("lmisc", "lx0", "lx1", "lact", "lact1", "lf0", "lf1",
                     "so0", "so1", "gw0", "gw1", "gwl", "gmisc",
                     "gs0", "gs1", "sa", "sa1", "v", "s", "t", "cc", "g",
                     "gck", "gcv", "gcr", "gco", "gc0", "gc1", "gcl"):
            sems[name] = ctx.enter_context(nc.semaphore("sem_" + name))

        al = ctx.enter_context

        xnT8 = al(nc.sbuf_tensor("xnT8", [128, G, B, CT], F8))
        tmk_s = al(nc.sbuf_tensor("tmk_s", [128, G], F32))
        tmv_s = al(nc.sbuf_tensor("tmv_s", [128, G], F32))
        tmr_s = al(nc.sbuf_tensor("tmr_s", [128, G], F32))
        omtk = al(nc.sbuf_tensor("omtk", [128, G], F32))
        omtv = al(nc.sbuf_tensor("omtv", [128, G], F32))
        omtr = al(nc.sbuf_tensor("omtr", [128, G], F32))
        td_s = al(nc.sbuf_tensor("td_s", [128, G], F32))
        tf_s = al(nc.sbuf_tensor("tf_s", [128, G], F32))
        etd_s = al(nc.sbuf_tensor("etd_s", [128, G], F32))
        ntd_s = al(nc.sbuf_tensor("ntd_s", [128, G], F32))
        wdec = al(nc.sbuf_tensor("wdec", [128, G], F32))
        etf_s = al(nc.sbuf_tensor("etf_s", [128, G], F32))
        ecnt_s = al(nc.sbuf_tensor("ecnt_s", [128, 8], F32))
        ident_s = al(nc.sbuf_tensor("ident_s", [128, 128], F32))
        wmt = al(nc.sbuf_tensor("wmt", [128, 8 * G], F32))
        wm_s = al(nc.sbuf_tensor("wm_s", [128, 8 * G], F32))
        nhb = al(nc.sbuf_tensor("nhb", [128, 1], F32))
        halfb = al(nc.sbuf_tensor("halfb", [128, 1], F32))
        stats = al(nc.sbuf_tensor("stats", [128, 24], F32))
        afbf = al(nc.sbuf_tensor("afbf", [128, 32], F32))
        g_sb = al(nc.sbuf_tensor("g_sb", [128, 8, 32], F32))
        sab = al(nc.sbuf_tensor("sab", [128, 32], F32))
        acca = al(nc.sbuf_tensor("acca", [128, G], F32))
        accb = al(nc.sbuf_tensor("accb", [128, G], F32))
        tmpm = al(nc.sbuf_tensor("tmpm", [128, G], F32))

        wg0st = [al(nc.sbuf_tensor("wg0st%d" % i, [128, G, 128], F8))
                 for i in range(2)]
        wg1st = [al(nc.sbuf_tensor("wg1st%d" % i, [128, G, 128], F8))
                 for i in range(2)]
        ffg = al(nc.sbuf_tensor("ffg", [128, 512], BF16))
        ff1 = al(nc.sbuf_tensor("ff1", [128, 512], BF16))
        gt = [al(nc.sbuf_tensor("gt%d" % i, [128, 512], BF16)) for i in range(2)]
        act8 = [al(nc.sbuf_tensor("act8_%d" % i, [128, 512], F8))
                for i in range(2)]
        ga = al(nc.sbuf_tensor("ga", [128, 512], BF16))

        psA = al(nc.psum_tensor("psA", [128, 2048], F32))
        psB = al(nc.psum_tensor("psB", [128, 2048], F32))
        pb = [psA[:, i * 512:(i + 1) * 512] for i in range(4)] + \
             [psB[:, i * 512:(i + 1) * 512] for i in range(4)]

        for sb, dd in ((tmk_s, tmk_d), (tmv_s, tmv_d), (tmr_s, tmr_d),
                       (td_s, td_d), (tf_s, tf_d), (ecnt_s, ecnt_d),
                       (ident_s, ident_d)):
            dma("sync", "lmisc", sb[:, :], dd[:, :])
        t_par = cnt["lmisc"]

        cast_done = {}
        cast_state = {"prev": None}

        def emit_cast(sem, dst, srcw):
            prev = cast_state["prev"]
            if prev is not None:
                W("gpsimd", prev, cast_done[prev])
            step = dst.shape[0] // 4
            for i in range(4):
                cast_done[sem] = dma("gpsimd", sem,
                                     dst[i * step:(i + 1) * step],
                                     srcw[i * step:(i + 1) * step])
            cast_state["prev"] = sem

        emit_cast("gc0", wg0b_d, wg0_d)
        emit_cast("gc1", wg1b_d, wg1_d)
        emit_cast("gco", wob_d, wo_d)
        emit_cast("gck", wkb_d, wk_d)
        emit_cast("gcv", wvb_d, wv_d)
        emit_cast("gcr", wrb_d, wr_d)

        W("vector", "lmisc", t_par)
        opn("vector", lambda e: e.memset(nhb[:, :], NEG_LN2))
        opn("vector", lambda e: e.memset(halfb[:, :], 0.5))
        for tm, om in ((tmk_s, omtk), (tmv_s, omtv), (tmr_s, omtr)):
            opn("vector", lambda e, tm=tm, om=om:
                e.tensor_scalar(om[:, :], tm[:, :], -1.0, 1.0, ALU.mult, ALU.add))
        W("scalar", "lmisc", t_par)
        t_etd = op("scalar", "s", lambda e: e.activation(etd_s[:, :], td_s[:, :],
                                                         AF.Exp))
        W("vector", "s", t_etd)
        t_ntd = op("vector", "v", lambda e:
                   e.tensor_scalar_mul(ntd_s[:, :], etd_s[:, :], -1.0))
        W("scalar", "v", t_ntd)
        opn("scalar", lambda e: e.activation(wdec[:, :], ntd_s[:, :], AF.Exp))
        t_wdec = op("scalar", "s", lambda e: e.activation(etf_s[:, :], tf_s[:, :],
                                                          AF.Exp))
        W("vector", "s", t_wdec)
        for m in range(8):
            bld = (lambda e, m=m: e.tensor_scalar(
                wmt[:, m * G:(m + 1) * G], ntd_s[:, :], ecnt_s[:, m:m + 1],
                None, ALU.mult))
            if m < 7:
                opn("vector", bld)
            else:
                t_wmt = op("vector", "v", bld)
        W("scalar", "v", t_wmt)
        t_wm = op("scalar", "s", lambda e: e.activation(wm_s[:, :], wmt[:, :],
                                                        AF.Exp))

        attn_ctx = ExitStack()
        aa = attn_ctx.enter_context
        xnT = aa(nc.sbuf_tensor("xnT", [128, G, CT + 1], F32))
        mixT = aa(nc.sbuf_tensor("mixT", [128, G, CT], BF16))
        ek = aa(nc.sbuf_tensor("ek", [128, G, CT], F32))
        vT = aa(nc.sbuf_tensor("vT", [128, G, CT], BF16))
        tanhr = aa(nc.sbuf_tensor("tanhr", [128, G, CT], BF16))
        wst = [aa(nc.sbuf_tensor("wst%d" % i, [128, G, 128], BF16))
               for i in range(2)]
        xtile = [aa(nc.sbuf_tensor("xtile0", [128, H], F32))]
        xn_t = aa(nc.sbuf_tensor("xn_t", [128, H // 2], F32))
        atr = aa(nc.sbuf_tensor("atr", [128, CT + 1], F32))
        btr = aa(nc.sbuf_tensor("btr", [128, CT + 1], F32))
        s1 = [aa(nc.sbuf_tensor("s1_%d" % i, [128, CT], F32)) for i in range(2)]
        s5 = aa(nc.sbuf_tensor("s5", [128, CT], F32))
        s2 = [aa(nc.sbuf_tensor("s2_%d" % i, [128, CT], F32)) for i in range(2)]
        s3 = [aa(nc.sbuf_tensor("s3_0", [128, CT], F32))] * 2
        s4 = [aa(nc.sbuf_tensor("s4_%d" % i, [128, CT], F32)) for i in range(2)]
        ekv_s = aa(nc.sbuf_tensor("ekv_s", [128, CT], F32))
        attw = [aa(nc.sbuf_tensor("attw%d" % i, [128, 512], BF16))
                for i in range(2)]

        xt_guard = {0: (0, 0)}
        gw_guard = {0: 0, 1: 0}
        bank_guard = {}
        xn_hist = {0: 0, 1: 0, 2: 0}

        def bank_wait(eng, idx):
            if idx in bank_guard:
                sname, v = bank_guard[idx]
                W(eng, sname, v)

        R0S = [0, 128, 256, 384, 385]

        ffn_state = dict(gate=False, a8b=0)
        ffn_nxt = [0] * NJ
        a8_guard = {0: 0, 1: 0}
        th_prev = [0]
        act_prev = [0]
        xnT8_ready = {}

        def ffn12_drain(upto_b, max_units=None):
            emitted = 0
            for j in range(NJ):
                if ffn_nxt[j] > upto_b:
                    continue
                if max_units is not None and emitted >= max_units:
                    break
                bbs = list(range(ffn_nxt[j], upto_b + 1))
                if not ffn_state["gate"]:
                    W("sync", "gc0", cast_done["gc0"])
                    W("sync", "gc1", cast_done["gc1"])
                    ffn_state["gate"] = True
                bufi = j % 2
                t_base = cnt["t"]
                pred_last = t_base + len(bbs)
                W("sync", "t", gw_guard[bufi])
                gws = "gw%d" % bufi
                dma("sync", gws, wg0st[bufi][:, :, :], wg0b_d[j, :, :, :])
                wg_ld = dma("sync", gws, wg1st[bufi][:, :, :],
                            wg1b_d[j, :, :, :])
                gw_guard[bufi] = pred_last
                for bb in bbs:
                    bkF, bkG = 2, 3
                    W("tensor", gws, wg_ld)
                    W("tensor", "v", xnT8_ready[bb])
                    bank_wait("tensor", bkF)
                    bank_wait("tensor", bkG)
                    for g in range(G):
                        opn("tensor", lambda e, g=g, bi=bufi, bb=bb, bk=bkF:
                            e.matmul(pb[bk][:, :], wg0st[bi][:, g, :],
                                     xnT8[:, g, bb, :], start=(g == 0),
                                     stop=(g == G - 1)))
                    for g in range(G):
                        bld = (lambda e, g=g, bi=bufi, bb=bb, bk=bkG: e.matmul(
                            pb[bk][:, :], wg1st[bi][:, g, :],
                            xnT8[:, g, bb, :], start=(g == 0),
                            stop=(g == G - 1)))
                        if g < G - 1:
                            opn("tensor", bld)
                        else:
                            t_mm = op("tensor", "t", bld)
                    a8b = ffn_state["a8b"]
                    W("scalar", "t", t_mm)
                    W("scalar", "g", act_prev[0])
                    t_efg = op("scalar", "s", lambda e, bk=bkF: e.activation(
                        ffg[:, :], pb[bk][:, :], AF.Identity))
                    W("vector", "t", t_mm)
                    W("vector", "g", act_prev[0])
                    evb = op("vector", "v", lambda e, bk=bkG: e.tensor_copy(
                        ff1[:, :], pb[bk][:, :]))
                    bank_guard[bkF] = ("s", t_efg)
                    bank_guard[bkG] = ("v", evb)
                    W("gpsimd", "s", t_efg)
                    opn("gpsimd", lambda e: e.tensor_mul(
                        gt[0][:, :], ffg[:, :], ffg[:, :]))
                    t_x3 = op("gpsimd", "g", lambda e: e.tensor_mul(
                        gt[1][:, :], gt[0][:, :], ffg[:, :]))
                    W("vector", "g", t_x3)
                    opn("vector", lambda e: e.tensor_scalar_mul(
                        gt[0][:, :], gt[1][:, :], C1))
                    opn("vector", lambda e: e.tensor_add(
                        gt[1][:, :], ffg[:, :], gt[0][:, :]))
                    t_g5 = op("vector", "v", lambda e: e.tensor_scalar_mul(
                        gt[0][:, :], gt[1][:, :], C2))
                    W("scalar", "v", t_g5)
                    opn("scalar", lambda e: e.activation(
                        gt[1][:, :], gt[0][:, :], AF.Tanh))
                    t_th = op("scalar", "s", lambda e: e.activation(
                        gt[0][:, :], gt[1][:, :], AF.Identity, scale=0.5,
                        bias=halfb[:, 0:1]))
                    th_prev[0] = t_th
                    W("gpsimd", "s", t_th)
                    opn("gpsimd", lambda e: e.tensor_mul(
                        gt[1][:, :], ffg[:, :], gt[0][:, :]))
                    t_act = op("gpsimd", "g", lambda e: e.tensor_mul(
                        ga[:, :], gt[1][:, :], ff1[:, :]))
                    act_prev[0] = t_act
                    W("scalar", "g", t_act)
                    W("scalar", "gs%d" % a8b, a8_guard[a8b])
                    t_a8 = op("scalar", "s", lambda e, a=a8b: e.activation(
                        act8[a][:, :], ga[:, :], AF.Identity))
                    W("gpsimd", "s", t_a8)
                    a8_guard[a8b] = dma("gpsimd", "gs%d" % a8b,
                                        act_d[bb, j * 128:(j + 1) * 128, :],
                                        act8[a8b][:, :])
                    ffn_state["a8b"] ^= 1
                    emitted += 1
                assert cnt["t"] == pred_last
                ffn_nxt[j] = upto_b + 1
            return emitted

        wo_mm_last = 0
        tr_last_t = 0
        xn_cnt_hist = [0, 0]
        at_guard = {0: 0}
        st = {"tr_last_t": 0, "t_sq_last": 0, "wo_mm_last": 0,
              "t_rrwkv_prev": 0}

        def emit_prep(b):
            for ti, r0 in enumerate(R0S):
                buf = 0
                gv, gsc = xt_guard[buf]
                W("gpsimd", "v", gv)
                W("gpsimd", "s", gsc)
                lxs = "lx%d" % (ti % 2)
                t_x = dma("gpsimd", lxs, xtile[buf][:, :],
                          x_d[b, r0:r0 + 128, :])
                W("scalar", lxs, t_x)
                W("scalar", "v", xn_cnt_hist[ti % 2])
                pc = 16 if ti % 2 == 0 else 20
                accol = 1 if ti % 2 == 0 else 13
                sumcol = 0 if ti % 2 == 0 else 14
                for half in range(2):
                    opn("scalar", lambda e, xa=xtile[buf], h=half, c=pc:
                        e.activation(psB[:, 1024:2048],
                                     xa[:, h * 1024:(h + 1) * 1024],
                                     AF.Identity,
                                     accum_out=stats[:, c + h:c + h + 1]))
                for half in range(2):
                    t_sq = op("scalar", "s",
                              lambda e, xa=xtile[buf], h=half, c=pc:
                              e.activation(psB[:, 1024:2048],
                                           xa[:, h * 1024:(h + 1) * 1024],
                                           AF.Square,
                                           accum_out=stats[:, c + 2 + h:c + 3 + h]))
                st["t_sq_last"] = t_sq
                W("vector", "s", t_sq)
                opn("vector", lambda e, c=pc, sc=sumcol: e.tensor_add(
                    stats[:, sc:sc + 1], stats[:, c:c + 1], stats[:, c + 1:c + 2]))
                opn("vector", lambda e, c=pc, sc=accol: e.tensor_add(
                    stats[:, sc:sc + 1], stats[:, c + 2:c + 3],
                    stats[:, c + 3:c + 4]))
                opn("vector", lambda e, c=sumcol: e.tensor_scalar_mul(
                    stats[:, 2:3], stats[:, c:c + 1], 1.0 / H))
                opn("vector", lambda e, c=accol: e.tensor_scalar_mul(
                    stats[:, 3:4], stats[:, c:c + 1], 1.0 / H))
                opn("vector", lambda e: e.tensor_mul(
                    stats[:, 4:5], stats[:, 2:3], stats[:, 2:3]))
                opn("vector", lambda e: e.tensor_sub(
                    stats[:, 5:6], stats[:, 3:4], stats[:, 4:5]))
                opn("vector", lambda e: e.tensor_scalar_add(
                    stats[:, 6:7], stats[:, 5:6], 1e-5))
                opn("vector", lambda e: e.tensor_scalar(
                    stats[:, 7:8], stats[:, 6:7], -0.5, 1.5, ALU.mult, ALU.add))
                opn("vector", lambda e: e.tensor_scalar_max(
                    stats[:, 8:9], stats[:, 7:8], 0.25))
                cur, alt = 8, 12
                for _ in range(3):
                    opn("vector", lambda e, c=cur: e.tensor_mul(
                        stats[:, 9:10], stats[:, c:c + 1], stats[:, c:c + 1]))
                    opn("vector", lambda e: e.tensor_mul(
                        stats[:, 10:11], stats[:, 9:10], stats[:, 6:7]))
                    opn("vector", lambda e: e.tensor_scalar(
                        stats[:, 11:12], stats[:, 10:11], -0.5, 1.5,
                        ALU.mult, ALU.add))
                    opn("vector", lambda e, c=cur, a=alt: e.tensor_mul(
                        stats[:, a:a + 1], stats[:, c:c + 1], stats[:, 11:12]))
                    cur, alt = alt, cur
                t_xn = 0
                for hh in range(2):
                    W("vector", "t", st["tr_last_t"])
                    t_xn = op("vector", "v",
                              lambda e, xa=xtile[buf], c=cur, hh=hh:
                              e.tensor_scalar(
                                  xn_t[:, :],
                                  xa[:, hh * 1024:(hh + 1) * 1024],
                                  stats[:, 2:3], stats[:, c:c + 1],
                                  ALU.subtract, ALU.mult))
                    for gq in range(2):
                        bidx = gq % 2
                        bank_wait("tensor", bidx)
                        W("tensor", "v", t_xn)
                        for q in range(4):
                            g = gq * 4 + q
                            bld = (lambda e, g=g, q=q, bi=bidx: e.transpose(
                                pb[bi][:, q * 128:(q + 1) * 128],
                                xn_t[:, g * 128:(g + 1) * 128], ident_s[:, :]))
                            if q < 3:
                                opn("tensor", bld)
                            else:
                                t_tr = op("tensor", "t", bld)
                        st["tr_last_t"] = t_tr
                        W("vector", "t", t_tr)
                        gq8 = hh * 2 + gq
                        ev = op("vector", "v",
                                lambda e, gq8=gq8, bi=bidx, r0=r0:
                                e.tensor_copy(
                                    xnT[:, gq8 * 4:gq8 * 4 + 4, r0:r0 + 128],
                                    pb[bi].rearrange("p (i c) -> p i c", i=4)))
                        bank_guard[bidx] = ("v", ev)
                xt_guard[buf] = (t_xn, t_sq)
                xn_cnt_hist[ti % 2] = t_xn
            for bk in (6, 7):
                bank_guard[bk] = ("s", st["t_sq_last"])
            t_xnT8 = op("vector", "v", lambda e, b=b: e.tensor_copy(
                xnT8[:, :, b, :], xnT[:, :, 1:CT + 1]))
            xnT8_ready[b] = t_xnT8

        emit_prep(0)
        for b in range(B):
            proj_specs = ((tmk_s, omtk, wkb_d, wk_d, "gck"),
                          (tmv_s, omtv, wvb_d, wv_d, "gcv"),
                          (tmr_s, omtr, wrb_d, wr_d, "gcr"))
            t_prev_w_mm = wo_mm_last
            ek_done = tr_done = vt_done = 0
            for w in range(3):
                tm_sb, om_sb, W_dram, W_f32, csem = proj_specs[w]
                W("vector", "t", t_prev_w_mm)
                for g in range(G):
                    opn("vector", lambda e, g=g, tm=tm_sb: e.tensor_scalar(
                        s1[0][:, :], xnT[:, g, 1:CT + 1], tm[:, g:g + 1], None,
                        ALU.mult))
                    bld = (lambda e, g=g, om=om_sb: e.scalar_tensor_tensor(
                        mixT[:, g, :], xnT[:, g, 0:CT], om[:, g:g + 1],
                        s1[0][:, :], ALU.mult, ALU.add))
                    if g < G - 1:
                        opn("vector", bld)
                    else:
                        t_mix = op("vector", "v", bld)

                t_base = cnt["t"]
                pred = {h: t_base + h + 1 for h in range(G)}
                wld = []
                if b == 0:
                    W("sync", csem, cast_done[csem])
                for h in range(G):
                    bufi = h % 2
                    W("sync", "t", gw_guard[bufi])
                    gws = "gw%d" % bufi
                    wld.append((gws, dma("sync", gws, wst[bufi][:, :, :],
                                         W_dram[h, :, :, :])))
                    gw_guard[bufi] = pred[h]
                for h in range(G):
                    bufi = h % 2
                    bidx = w if h % 2 == 0 else w + 4
                    W("tensor", wld[h][0], wld[h][1])
                    if h == 0:
                        W("tensor", "v", t_mix)
                    bank_wait("tensor", bidx)
                    for g in range(G):
                        bld = (lambda e, g=g, bi=bufi, bk=bidx: e.matmul(
                            pb[bk][:, :], wst[bi][:, g, :], mixT[:, g, :],
                            start=(g == 0), stop=(g == G - 1)))
                        if g < G - 1:
                            opn("tensor", bld)
                        else:
                            t_mm = op("tensor", "t", bld)
                    assert t_mm == pred[h]
                    if w == 0:
                        W("scalar", "t", t_mm)
                        ev = op("scalar", "s", lambda e, h=h, bk=bidx:
                                e.activation(ek[:, h, :], pb[bk][:, :], AF.Exp))
                        bank_guard[bidx] = ("s", ev)
                        ek_done = ev
                    elif w == 1:
                        W("vector", "t", t_mm)
                        ev = op("vector", "v", lambda e, h=h, bk=bidx:
                                e.tensor_copy(vT[:, h, :], pb[bk][:, :]))
                        bank_guard[bidx] = ("v", ev)
                        vt_done = ev
                    else:
                        W("scalar", "t", t_mm)
                        ev = op("scalar", "s", lambda e, h=h, bk=bidx:
                                e.activation(tanhr[:, h, :], pb[bk][:, :],
                                             AF.Tanh, scale=0.5))
                        bank_guard[bidx] = ("s", ev)
                        tr_done = ev
                t_prev_w_mm = t_mm
            t_proj_mm_last = t_prev_w_mm

            W("vector", "s", ek_done)
            for g in range(G):
                wb = wdec[:, g:g + 1].broadcast_to([128, CT])
                opn("vector", lambda e, g=g: e.tensor_mul(
                    ekv_s[:, :], ek[:, g, :], vT[:, g, :]))
                opn("vector", lambda e, wb=wb: e.tensor_tensor_scan(
                    atr[:, 1:CT + 1], wb, ekv_s[:, :], 0.0, ALU.mult, ALU.add))
                opn("vector", lambda e, g=g: e.tensor_copy(
                    afbf[:, g:g + 1], atr[:, CT:CT + 1]))
                opn("vector", lambda e, g=g, wb=wb: e.tensor_tensor_scan(
                    btr[:, 1:CT + 1], wb, ek[:, g, :], 0.0, ALU.mult, ALU.add))
                bld = (lambda e, g=g: e.tensor_copy(
                    afbf[:, G + g:G + g + 1], btr[:, CT:CT + 1]))
                if g < G - 1:
                    opn("vector", bld)
                else:
                    t_scan = op("vector", "v", bld)

            W("gpsimd", "v", t_scan)
            t_agin = dma("gpsimd", "gmisc", agin[b][:, :], afbf[:, :])
            W("gpsimd", "gmisc", t_agin)
            cnt["cc"] += 1
            emit("gpsimd", lambda e, b=b: e.collective_compute(
                "AllGather", ALU.bypass, replica_groups=[list(range(NCORE))],
                ins=[agin[b].ap().opt()], outs=[agout[b].ap().opt()],
            ).then_inc(sems["cc"], 1))
            t_cc = cnt["cc"]
            if b == 0:
                emit_cast("gcl", wlb_d, wl_d)
            ffn12_drain(b, max_units=40)
            if b + 1 < B:
                emit_prep(b + 1)

            W("gpsimd", "cc", t_cc)
            gre = agout[b].ap().rearrange("(m p) c -> p m c", p=128)
            dma("gpsimd", "gmisc", g_sb[:, 0:4, :], gre[:, 0:4, :])
            t_G = dma("gpsimd", "gmisc", g_sb[:, 4:8, :], gre[:, 4:8, :])

            W("vector", "gmisc", t_G)
            W("vector", "s", t_wm)
            for q in range(2):
                opn("vector", lambda e, q=q: e.tensor_mul(
                    acca[:, :], g_sb[:, 0, q * G:(q + 1) * G], wm_s[:, 0:G]))
                accs = [acca, accb]
                src = acca
                for m in range(1, 8):
                    opn("vector", lambda e, q=q, m=m: e.tensor_mul(
                        tmpm[:, :], g_sb[:, m, q * G:(q + 1) * G],
                        wm_s[:, m * G:(m + 1) * G]))
                    if m == 7:
                        opn("vector", lambda e, q=q, sc=src: e.tensor_add(
                            sab[:, q * G:(q + 1) * G], sc[:, :], tmpm[:, :]))
                    else:
                        dst = accs[m % 2]
                        opn("vector", lambda e, d=dst, sc=src: e.tensor_add(
                            d[:, :], sc[:, :], tmpm[:, :]))
                        src = dst
            W("vector", "t", t_proj_mm_last)
            t_rec_by_g = {}
            den_by_g = {-1: 0}

            def out_phase1(g):
                wb = wdec[:, g:g + 1].broadcast_to([128, CT])
                par = g % 2
                W("gpsimd", "v", den_by_g[g - 1])
                t_ekv = op("gpsimd", "g", lambda e, g=g: e.tensor_mul(
                    ekv_s[:, :], ek[:, g, :], vT[:, g, :]))
                W("vector", "g", t_ekv)
                opn("vector", lambda e, g=g, wb=wb: e.tensor_tensor_scan(
                    atr[:, 1:CT + 1], wb, ekv_s[:, :], sab[:, g:g + 1],
                    ALU.mult, ALU.add))
                opn("vector", lambda e, g=g: e.tensor_copy(
                    atr[:, 0:1], sab[:, g:g + 1]))
                opn("vector", lambda e, g=g, wb=wb: e.tensor_tensor_scan(
                    btr[:, 1:CT + 1], wb, ek[:, g, :],
                    sab[:, G + g:G + g + 1], ALU.mult, ALU.add))
                opn("vector", lambda e, g=g: e.tensor_copy(
                    btr[:, 0:1], sab[:, G + g:G + g + 1]))
                num_by_g[g] = op("vector", "v", lambda e, g=g, p=par:
                                 e.scalar_tensor_tensor(
                    s1[p][:, :], ekv_s[:, :], etf_s[:, g:g + 1], atr[:, 0:CT],
                    ALU.mult, ALU.add))
                t_den = op("vector", "v", lambda e, g=g, p=par:
                           e.scalar_tensor_tensor(
                    s2[p][:, :], ek[:, g, :], etf_s[:, g:g + 1], btr[:, 0:CT],
                    ALU.mult, ALU.add))
                den_by_g[g] = t_den
                W("scalar", "v", t_den)
                opn("scalar", lambda e, p=par: e.activation(
                    s3[p][:, :], s2[p][:, :], AF.Ln))
                t_rec_by_g[g] = op("scalar", "s", lambda e, p=par: e.activation(
                    s4[p][:, :], s3[p][:, :], AF.Exp, bias=nhb[:, 0:1],
                    scale=-1.0))

            num_by_g = {}

            rr_prev = [0]

            def out_phase2(g, last):
                par = g % 2
                W("gpsimd", "s", t_rec_by_g[g])
                W("gpsimd", "v", num_by_g[g])
                W("gpsimd", "v", rr_prev[0])
                t_wkv = op("gpsimd", "g", lambda e, p=par: e.tensor_mul(
                    s5[:, :], s1[p][:, :], s4[p][:, :]))
                W("vector", "g", t_wkv)
                t_rr = op("vector", "v", lambda e, g=g: e.scalar_tensor_tensor(
                    mixT[:, g, :], tanhr[:, g, :], 1.0, s5[:, :],
                    ALU.add, ALU.mult))
                rr_prev[0] = t_rr
                return t_rr

            out_phase1(0)
            for g in range(1, G):
                out_phase1(g)
                out_phase2(g - 1, False)
            t_rrwkv = out_phase2(G - 1, True)
            st["t_rrwkv_prev"] = t_rrwkv

            banks = [0, 1, 4, 5]
            for ob in range(4):
                t_base = cnt["t"]
                pred = {h: t_base + h + 1 for h in range(G)}
                woldc = []
                if b == 0 and ob == 0:
                    W("sync", "gco", cast_done["gco"])
                for h in range(G):
                    bufi = h % 2
                    W("sync", "t", gw_guard[bufi])
                    gws = "gw%d" % bufi
                    woldc.append((gws, dma(
                        "sync", gws,
                        wst[bufi].ap().rearrange("p g c -> p (g c)")[:, 0:512],
                        wob_d[h, ob, :, :])))
                    gw_guard[bufi] = pred[h]
                for h in range(G):
                    bufi = h % 2
                    W("tensor", woldc[h][0], woldc[h][1])
                    if h == 0:
                        W("tensor", "v", t_rrwkv)
                        for bk in banks:
                            bank_wait("tensor", bk)
                    for tsub in range(4):
                        bld = (lambda e, h=h, ts=tsub, bi=bufi, bk=banks[tsub]:
                               e.matmul(
                                   pb[bk][:, :],
                                   mixT[:, h, ts * 128:(ts + 1) * 128],
                                   wst[bi].ap().rearrange(
                                       "p g c -> p (g c)")[:, 0:512],
                                   start=(h == 0), stop=(h == G - 1)))
                        if tsub < 3:
                            opn("tensor", bld)
                        else:
                            t_mm = op("tensor", "t", bld)
                    assert t_mm == pred[h]
                for tsub in range(4):
                    ai = tsub % 2
                    W("vector", "t", t_mm)
                    W("vector", "sa" if ai == 0 else "sa1",
                      at_guard.get(ai, 0))
                    tev = op("vector", "v", lambda e, bk=banks[tsub], ai=ai:
                             e.tensor_copy(attw[ai][:, :], pb[bk][:, :]))
                    bank_guard[banks[tsub]] = ("v", tev)
                    W("gpsimd", "v", tev)
                    at_guard[ai] = dma("gpsimd", "sa" if ai == 0 else "sa1",
                                       attn_d[b, tsub * 128:(tsub + 1) * 128,
                                              ob * 512:(ob + 1) * 512],
                                       attw[ai][:, :])
                wo_mm_last = t_mm
            ffn12_drain(b)

        ffn12_drain(B - 1)
        attn_ctx.close()

        ffn_ctx = ExitStack()
        fa = ffn_ctx.enter_context
        wlsb = fa(nc.sbuf_tensor("wlsb", [128, NJ, 1024], F8))
        actsb = [fa(nc.sbuf_tensor("actsb%d" % i, [128, NJ, 512], F8))
                 for i in range(2)]
        f1t = [fa(nc.sbuf_tensor("f1t%d" % i, [128, 512], BF16))
               for i in range(2)]
        xrt = [fa(nc.sbuf_tensor("xrt%d" % i, [128, 512], F32))
               for i in range(2)]
        att2 = [fa(nc.sbuf_tensor("att2_%d" % i, [128, 512], BF16))
                for i in range(2)]
        o1t = [fa(nc.sbuf_tensor("o1t%d" % i, [128, 512], F32))
               for i in range(2)]
        outt = [fa(nc.sbuf_tensor("outt%d" % i, [128, 512], F32))
                for i in range(2)]

        t_act_s0, t_act_s1 = cnt["gs0"], cnt["gs1"]
        t_attn_sa = cnt["sa"]
        t_attn_sa1 = cnt["sa1"]

        t_ffn3_mm = 0
        ev_guard = {}
        ob2 = 0
        out_guard = {0: 0, 1: 0}
        xr_guard = {0: 0, 1: 0}
        W("sync", "sa", t_attn_sa)
        W("sync", "sa1", t_attn_sa1)
        W("sync", "gcl", cast_done["gcl"])
        DR = mybir.MatmulPerfMode.DoubleRow
        act_ld_cnt = {0: 0, 1: 0}
        mm_done = {}
        for obh in range(2):
            W("sync", "t", t_ffn3_mm)
            wl_ld = 0
            for j in range(NJ):
                wl_ld = dma("sync", "gwl", wlsb[:, j, :],
                            wlb_d[j * 128:(j + 1) * 128,
                                  obh * 1024:(obh + 1) * 1024])
            for bb in range(B):
                w = obh * B + bb
                par = w % 2
                if act_ld_cnt[par] == 0 or True:
                    W("gpsimd", "gs0", t_act_s0)
                    W("gpsimd", "gs1", t_act_s1)
                    W("gpsimd", "t", mm_done.get(w - 2, 0))
                    lsem = "lact" if par == 0 else "lact1"
                    for j0 in range(0, NJ, 8):
                        act_ld_cnt[par] = dma(
                            "gpsimd", lsem, actsb[par][:, j0:j0 + 8, :],
                            act_d[bb, j0 * 128:(j0 + 8) * 128, :]
                            .rearrange("(j p) t -> p j t", p=128))
                W("tensor", "gwl", wl_ld)
                if obh == 0 and bb == 0:
                    for bk in range(8):
                        bank_wait("tensor", bk)
                W("tensor", "lact" if par == 0 else "lact1", act_ld_cnt[par])
                t_half = {}
                for half in range(2):
                    tsubs = (0, 1) if half == 0 else (2, 3)
                    for ts_ in tsubs:
                        for ot in range(2):
                            W("tensor", "v", ev_guard.get(ts_ * 2 + ot, 0))
                    for jp in range(NJ // 2):
                        for ts_ in tsubs:
                            for ot in range(2):
                                bld = (lambda e, jp=jp, ts=ts_, ot=ot, p2=par:
                                       e.matmul(
                                    pb[ts * 2 + ot][:, :],
                                    actsb[p2][:, 2 * jp:2 * jp + 2,
                                              ts * 128:(ts + 1) * 128],
                                    wlsb[:, 2 * jp:2 * jp + 2,
                                         ot * 512:(ot + 1) * 512],
                                    start=(jp == 0),
                                    stop=(jp == NJ // 2 - 1),
                                    perf_mode=DR))
                                if (jp == NJ // 2 - 1 and ts_ == tsubs[-1]
                                        and ot == 1):
                                    t_half[half] = op("tensor", "t", bld)
                                else:
                                    opn("tensor", bld)
                t_ffn3_mm = t_half[1]
                mm_done[w] = t_ffn3_mm
                for tsub in range(4):
                    for ot in range(2):
                        col0 = obh * 1024 + ot * 512
                        W("vector", "t", t_half[0 if tsub < 2 else 1])
                        tfv = op("vector", "v", lambda e, bk=tsub * 2 + ot,
                                 c=ob2: e.tensor_copy(f1t[c][:, :],
                                                      pb[bk][:, :]))
                        ev_guard[tsub * 2 + ot] = tfv
                        W("sync", "v", xr_guard[ob2])
                        lfs = "lf%d" % ob2
                        dma("sync", lfs, xrt[ob2][:, :],
                            x_d[bb, 1 + tsub * 128:1 + (tsub + 1) * 128,
                                col0:col0 + 512])
                        t_al = dma("sync", lfs, att2[ob2][:, :],
                                   attn_d[bb, tsub * 128:(tsub + 1) * 128,
                                          col0:col0 + 512])
                        W("vector", lfs, t_al)
                        W("vector", "so%d" % ob2, out_guard[ob2])
                        opn("vector", lambda e, c=ob2: e.tensor_add(
                            o1t[c][:, :], xrt[c][:, :], att2[c][:, :]))
                        t_ov = op("vector", "v", lambda e, c=ob2: e.tensor_add(
                            outt[c][:, :], o1t[c][:, :], f1t[c][:, :]))
                        xr_guard[ob2] = t_ov
                        W("sync", "v", t_ov)
                        out_guard[ob2] = dma(
                            "sync", "so%d" % ob2,
                            out_d[bb, tsub * 128:(tsub + 1) * 128,
                                  col0:col0 + 512],
                            outt[ob2][:, :])
                        ob2 ^= 1

        ffn_ctx.close()

        W("sync", "so0", cnt["so0"])
        W("sync", "so1", cnt["so1"])
        W("gpsimd", "gs0", cnt["gs0"])
        W("gpsimd", "gs1", cnt["gs1"])
        W("gpsimd", "sa", cnt["sa"])
        W("gpsimd", "sa1", cnt["sa1"])

        block = ctx.enter_context(nc.Block())

        @block.sync
        def _(e):
            for f in steps["sync"]:
                f(e)

        @block.gpsimd
        def _(e):
            for f in steps["gpsimd"]:
                f(e)

        @block.vector
        def _(e):
            for f in steps["vector"]:
                f(e)

        @block.scalar
        def _(e):
            for f in steps["scalar"]:
                f(e)

        @block.tensor
        def _(e):
            for f in steps["tensor"]:
                f(e)

    return nc


_NC_CACHE = []


def kernel(L0_attn_hidden, L0_attn_time_decay, L0_attn_time_first,
           L0_attn_time_mix_key, L0_attn_time_mix_value,
           L0_attn_time_mix_receptance, L0_attn_Wk, L0_attn_Wv, L0_attn_Wr,
           L0_attn_Wo, L0_ffn_w_gating, L0_ffn_w_linear):
    x = np.ascontiguousarray(np.asarray(L0_attn_hidden, dtype=np.float32))
    x_pad = np.concatenate([np.zeros((B, 1, H), np.float32), x], axis=1)

    def vec16(v):
        return np.ascontiguousarray(
            np.asarray(v, np.float32).reshape(-1).reshape(G, 128).T)

    def tile_proj(w):
        return np.ascontiguousarray(
            np.asarray(w, np.float32).reshape(G, 128, G, 128)
            .transpose(2, 1, 0, 3))

    wk = tile_proj(L0_attn_Wk)
    wv = tile_proj(L0_attn_Wv)
    wr = tile_proj(L0_attn_Wr)
    wo = np.ascontiguousarray(
        np.asarray(L0_attn_Wo, np.float32).reshape(G, 128, 4, 512)
        .transpose(0, 2, 1, 3))
    wg = np.asarray(L0_ffn_w_gating, np.float32)

    def tile_ffn(w):
        return np.ascontiguousarray(
            w.reshape(G, 128, NJ, 128).transpose(2, 1, 0, 3))

    wg0 = tile_ffn(wg[0])
    wg1 = tile_ffn(wg[1])
    wl = np.ascontiguousarray(np.asarray(L0_ffn_w_linear, np.float32))
    tmk = vec16(L0_attn_time_mix_key)
    tmv = vec16(L0_attn_time_mix_value)
    tmr = vec16(L0_attn_time_mix_receptance)
    td = vec16(L0_attn_time_decay)
    tf = vec16(L0_attn_time_first)
    ident = np.eye(128, dtype=np.float32)

    if not _NC_CACHE:
        _NC_CACHE.append(build_program())
    nc = _NC_CACHE[0]

    in_maps = []
    for c in range(NCORE):
        row = np.array([512.0 * (c - 1 - m) if m < c else 1e9
                        for m in range(8)], np.float32)
        ecnt = np.ascontiguousarray(np.broadcast_to(row, (128, 8)))
        xs = np.ascontiguousarray(x_pad[:, c * CT:c * CT + CT + 1, :])
        in_maps.append({
            "x": xs, "wk": wk, "wv": wv, "wr": wr, "wo": wo,
            "wg0": wg0, "wg1": wg1, "wl": wl,
            "tmk": tmk, "tmv": tmv, "tmr": tmr, "td": td, "tf": tf,
            "ecnt": ecnt, "ident": ident,
        })

    res = run_bass_kernel_spmd(nc, in_maps, core_ids=list(range(NCORE)))
    out = np.empty((B, T, H), np.float32)
    for c in range(NCORE):
        out[:, c * CT:(c + 1) * CT, :] = res.results[c]["out"]
    return out


# version 20 (final)
# speedup vs baseline: 1.1328x; 1.1328x over previous
import numpy as np
import ml_dtypes
from contextlib import ExitStack

import concourse.bass as bass
import concourse.mybir as mybir
from concourse.bass_utils import run_bass_kernel_spmd

F32 = mybir.dt.float32
BF16 = mybir.dt.bfloat16
F8 = mybir.dt.float8e4
ALU = mybir.AluOpType
AF = mybir.ActivationFunctionType
AX = mybir.AxisListType

B, T, H = 4, 4096, 2048
NCORE = 8
CT = T // NCORE
G = H // 128
JH = 8192
NJ = JH // 128

C1 = float(np.float32(ml_dtypes.bfloat16(0.044715)))
C2 = float(np.float32(ml_dtypes.bfloat16(np.sqrt(2.0 / np.pi))))
NEG_LN2 = -0.6931471805599453

ENGS = ("sync", "gpsimd", "vector", "scalar", "tensor")


def build_program():
    nc = bass.Bass("TRN2", target_bir_lowering=False)

    x_d = nc.dram_tensor("x", [B, CT + 1, H], F32, kind="ExternalInput")
    wk_d = nc.dram_tensor("wk", [G, 128, G, 128], F32, kind="ExternalInput")
    wv_d = nc.dram_tensor("wv", [G, 128, G, 128], F32, kind="ExternalInput")
    wr_d = nc.dram_tensor("wr", [G, 128, G, 128], F32, kind="ExternalInput")
    wo_d = nc.dram_tensor("wo", [G, 4, 128, 512], F32, kind="ExternalInput")
    wg0_d = nc.dram_tensor("wg0", [NJ, 128, G, 128], F32, kind="ExternalInput")
    wg1_d = nc.dram_tensor("wg1", [NJ, 128, G, 128], F32, kind="ExternalInput")
    wl_d = nc.dram_tensor("wl", [JH, H], F32, kind="ExternalInput")
    tmk_d = nc.dram_tensor("tmk", [128, G], F32, kind="ExternalInput")
    tmv_d = nc.dram_tensor("tmv", [128, G], F32, kind="ExternalInput")
    tmr_d = nc.dram_tensor("tmr", [128, G], F32, kind="ExternalInput")
    td_d = nc.dram_tensor("td", [128, G], F32, kind="ExternalInput")
    tf_d = nc.dram_tensor("tf", [128, G], F32, kind="ExternalInput")
    ecnt_d = nc.dram_tensor("ecnt", [128, 8], F32, kind="ExternalInput")
    ident_d = nc.dram_tensor("ident", [128, 128], F32, kind="ExternalInput")
    out_d = nc.dram_tensor("out", [B, CT, H], F32, kind="ExternalOutput")

    wkb_d = nc.dram_tensor("wkb", [G, 128, G, 128], BF16)
    wvb_d = nc.dram_tensor("wvb", [G, 128, G, 128], BF16)
    wrb_d = nc.dram_tensor("wrb", [G, 128, G, 128], BF16)
    wob_d = nc.dram_tensor("wob", [G, 4, 128, 512], BF16)
    wg0b_d = nc.dram_tensor("wg0b", [NJ, 128, G, 128], F8)
    wg1b_d = nc.dram_tensor("wg1b", [NJ, 128, G, 128], F8)
    wlb_d = nc.dram_tensor("wlb", [JH, H], F8)
    attn_d = nc.dram_tensor("attn_sc", [B, CT, H], BF16)
    act_d = nc.dram_tensor("act_sc", [B, JH, CT], F8)
    agin = [nc.dram_tensor("agin%d" % b, [128, 32], F32) for b in range(B)]
    agout = [nc.dram_tensor("agout%d" % b, [1024, 32], F32) for b in range(B)]

    steps = {k: [] for k in ENGS}
    cnt = dict(lmisc=0, lx0=0, lx1=0, lact=0, lact1=0, lf0=0, lf1=0,
               so0=0, so1=0, gw0=0, gw1=0, gwl=0, gmisc=0,
               gs0=0, gs1=0, sa=0, sa1=0, v=0, s=0, t=0, cc=0, g=0,
               gck=0, gcv=0, gcr=0, gco=0, gc0=0, gc1=0, gcl=0)
    sems = {}

    def emit(eng, f):
        steps[eng].append(f)

    def W(eng, sem, val):
        if val <= 0:
            return
        emit(eng, lambda e, sname=sem, v=val: e.wait_ge(sems[sname], v))

    def dma(eng, sem, out_ap, in_ap):
        cnt[sem] += 16
        emit(eng, lambda e, o=out_ap, i=in_ap, sname=sem:
             e.dma_start(out=o, in_=i).then_inc(sems[sname], 16))
        return cnt[sem]

    _OWN = {"vector": "v", "scalar": "s", "gpsimd": "g"}

    def op(eng, sem, bld):
        own = _OWN.get(eng)
        if own is not None:
            assert sem == own
            W(eng, own, cnt[own])
        cnt[sem] += 1
        emit(eng, lambda e, bd=bld, sname=sem: bd(e).then_inc(sems[sname], 1))
        return cnt[sem]

    def opn(eng, bld):
        own = _OWN.get(eng)
        if own is not None:
            return op(eng, own, bld)
        emit(eng, lambda e, bd=bld: bd(e))

    def mark(eng, sem):
        cnt[sem] += 1
        emit(eng, lambda e, sname=sem: e.nop().then_inc(sems[sname], 1))
        return cnt[sem]

    ctx = ExitStack()
    with ctx:
        for name in ("lmisc", "lx0", "lx1", "lact", "lact1", "lf0", "lf1",
                     "so0", "so1", "gw0", "gw1", "gwl", "gmisc",
                     "gs0", "gs1", "sa", "sa1", "v", "s", "t", "cc", "g",
                     "gck", "gcv", "gcr", "gco", "gc0", "gc1", "gcl"):
            sems[name] = ctx.enter_context(nc.semaphore("sem_" + name))

        al = ctx.enter_context

        xnT8 = al(nc.sbuf_tensor("xnT8", [128, G, B, CT], F8))
        tmk_s = al(nc.sbuf_tensor("tmk_s", [128, G], F32))
        tmv_s = al(nc.sbuf_tensor("tmv_s", [128, G], F32))
        tmr_s = al(nc.sbuf_tensor("tmr_s", [128, G], F32))
        omtk = al(nc.sbuf_tensor("omtk", [128, G], F32))
        omtv = al(nc.sbuf_tensor("omtv", [128, G], F32))
        omtr = al(nc.sbuf_tensor("omtr", [128, G], F32))
        td_s = al(nc.sbuf_tensor("td_s", [128, G], F32))
        tf_s = al(nc.sbuf_tensor("tf_s", [128, G], F32))
        etd_s = al(nc.sbuf_tensor("etd_s", [128, G], F32))
        ntd_s = al(nc.sbuf_tensor("ntd_s", [128, G], F32))
        wdec = al(nc.sbuf_tensor("wdec", [128, G], F32))
        etf_s = al(nc.sbuf_tensor("etf_s", [128, G], F32))
        ecnt_s = al(nc.sbuf_tensor("ecnt_s", [128, 8], F32))
        ident_s = al(nc.sbuf_tensor("ident_s", [128, 128], F32))
        wmt = al(nc.sbuf_tensor("wmt", [128, 8 * G], F32))
        wm_s = al(nc.sbuf_tensor("wm_s", [128, 8 * G], F32))
        nhb = al(nc.sbuf_tensor("nhb", [128, 1], F32))
        halfb = al(nc.sbuf_tensor("halfb", [128, 1], F32))
        stats = al(nc.sbuf_tensor("stats", [128, 24], F32))
        afbf = al(nc.sbuf_tensor("afbf", [128, 32], F32))
        g_sb = al(nc.sbuf_tensor("g_sb", [128, 8, 32], F32))
        sab = al(nc.sbuf_tensor("sab", [128, 32], F32))
        acca = al(nc.sbuf_tensor("acca", [128, G], F32))
        accb = al(nc.sbuf_tensor("accb", [128, G], F32))
        tmpm = al(nc.sbuf_tensor("tmpm", [128, G], F32))

        wg0st = [al(nc.sbuf_tensor("wg0st%d" % i, [128, G, 128], F8))
                 for i in range(2)]
        wg1st = [al(nc.sbuf_tensor("wg1st%d" % i, [128, G, 128], F8))
                 for i in range(2)]
        ffg = al(nc.sbuf_tensor("ffg", [128, 512], BF16))
        ff1 = al(nc.sbuf_tensor("ff1", [128, 512], BF16))
        gt = [al(nc.sbuf_tensor("gt%d" % i, [128, 512], BF16)) for i in range(2)]
        act8 = [al(nc.sbuf_tensor("act8_%d" % i, [128, 512], F8))
                for i in range(2)]
        ga = al(nc.sbuf_tensor("ga", [128, 512], BF16))

        psA = al(nc.psum_tensor("psA", [128, 2048], F32))
        psB = al(nc.psum_tensor("psB", [128, 2048], F32))
        pb = [psA[:, i * 512:(i + 1) * 512] for i in range(4)] + \
             [psB[:, i * 512:(i + 1) * 512] for i in range(4)]

        for sb, dd in ((tmk_s, tmk_d), (tmv_s, tmv_d), (tmr_s, tmr_d),
                       (td_s, td_d), (tf_s, tf_d), (ecnt_s, ecnt_d),
                       (ident_s, ident_d)):
            dma("sync", "lmisc", sb[:, :], dd[:, :])
        t_par = cnt["lmisc"]

        cast_done = {}
        cast_state = {"prev": None}

        def emit_cast(sem, dst, srcw):
            prev = cast_state["prev"]
            if prev is not None:
                W("gpsimd", prev, cast_done[prev])
            step = dst.shape[0] // 4
            for i in range(4):
                cast_done[sem] = dma("gpsimd", sem,
                                     dst[i * step:(i + 1) * step],
                                     srcw[i * step:(i + 1) * step])
            cast_state["prev"] = sem

        emit_cast("gc0", wg0b_d, wg0_d)
        emit_cast("gc1", wg1b_d, wg1_d)
        emit_cast("gco", wob_d, wo_d)
        emit_cast("gck", wkb_d, wk_d)
        emit_cast("gcv", wvb_d, wv_d)
        emit_cast("gcr", wrb_d, wr_d)

        W("vector", "lmisc", t_par)
        opn("vector", lambda e: e.memset(nhb[:, :], NEG_LN2))
        opn("vector", lambda e: e.memset(halfb[:, :], 0.5))
        for tm, om in ((tmk_s, omtk), (tmv_s, omtv), (tmr_s, omtr)):
            opn("vector", lambda e, tm=tm, om=om:
                e.tensor_scalar(om[:, :], tm[:, :], -1.0, 1.0, ALU.mult, ALU.add))
        W("scalar", "lmisc", t_par)
        t_etd = op("scalar", "s", lambda e: e.activation(etd_s[:, :], td_s[:, :],
                                                         AF.Exp))
        W("vector", "s", t_etd)
        t_ntd = op("vector", "v", lambda e:
                   e.tensor_scalar_mul(ntd_s[:, :], etd_s[:, :], -1.0))
        W("scalar", "v", t_ntd)
        opn("scalar", lambda e: e.activation(wdec[:, :], ntd_s[:, :], AF.Exp))
        t_wdec = op("scalar", "s", lambda e: e.activation(etf_s[:, :], tf_s[:, :],
                                                          AF.Exp))
        W("vector", "s", t_wdec)
        for m in range(8):
            bld = (lambda e, m=m: e.tensor_scalar(
                wmt[:, m * G:(m + 1) * G], ntd_s[:, :], ecnt_s[:, m:m + 1],
                None, ALU.mult))
            if m < 7:
                opn("vector", bld)
            else:
                t_wmt = op("vector", "v", bld)
        W("scalar", "v", t_wmt)
        t_wm = op("scalar", "s", lambda e: e.activation(wm_s[:, :], wmt[:, :],
                                                        AF.Exp))

        attn_ctx = ExitStack()
        aa = attn_ctx.enter_context
        xnT = aa(nc.sbuf_tensor("xnT", [128, G, CT + 1], F32))
        mixT = aa(nc.sbuf_tensor("mixT", [128, G, CT], BF16))
        ek = aa(nc.sbuf_tensor("ek", [128, G, CT], F32))
        vT = aa(nc.sbuf_tensor("vT", [128, G, CT], BF16))
        tanhr = aa(nc.sbuf_tensor("tanhr", [128, G, CT], BF16))
        wst = [aa(nc.sbuf_tensor("wst%d" % i, [128, G, 128], BF16))
               for i in range(2)]
        xtile = [aa(nc.sbuf_tensor("xtile0", [128, H], F32))]
        xn_t = aa(nc.sbuf_tensor("xn_t", [128, H // 2], F32))
        atr = aa(nc.sbuf_tensor("atr", [128, CT + 1], F32))
        btr = aa(nc.sbuf_tensor("btr", [128, CT + 1], F32))
        s1 = [aa(nc.sbuf_tensor("s1_%d" % i, [128, CT], F32)) for i in range(2)]
        s5 = aa(nc.sbuf_tensor("s5", [128, CT], F32))
        s2 = [aa(nc.sbuf_tensor("s2_%d" % i, [128, CT], F32)) for i in range(2)]
        s3 = [aa(nc.sbuf_tensor("s3_0", [128, CT], F32))] * 2
        s4 = [aa(nc.sbuf_tensor("s4_%d" % i, [128, CT], F32)) for i in range(2)]
        ekv_s = aa(nc.sbuf_tensor("ekv_s", [128, CT], F32))
        attw = [aa(nc.sbuf_tensor("attw%d" % i, [128, 512], BF16))
                for i in range(2)]

        xt_guard = {0: (0, 0)}
        gw_guard = {0: 0, 1: 0}
        bank_guard = {}
        xn_hist = {0: 0, 1: 0, 2: 0}

        def bank_wait(eng, idx):
            if idx in bank_guard:
                sname, v = bank_guard[idx]
                W(eng, sname, v)

        R0S = [0, 128, 256, 384, 385]

        ffn_state = dict(gate=False, a8b=0)
        ffn_nxt = [0] * NJ
        a8_guard = {0: 0, 1: 0}
        th_prev = [0]
        act_prev = [0]
        xnT8_ready = {}

        def ffn12_drain(upto_b, max_units=None):
            emitted = 0
            for j in range(NJ):
                if ffn_nxt[j] > upto_b:
                    continue
                if max_units is not None and emitted >= max_units:
                    break
                bbs = list(range(ffn_nxt[j], upto_b + 1))
                if not ffn_state["gate"]:
                    W("sync", "gc0", cast_done["gc0"])
                    W("sync", "gc1", cast_done["gc1"])
                    ffn_state["gate"] = True
                bufi = j % 2
                t_base = cnt["t"]
                pred_last = t_base + len(bbs)
                W("sync", "t", gw_guard[bufi])
                gws = "gw%d" % bufi
                dma("sync", gws, wg0st[bufi][:, :, :], wg0b_d[j, :, :, :])
                wg_ld = dma("sync", gws, wg1st[bufi][:, :, :],
                            wg1b_d[j, :, :, :])
                gw_guard[bufi] = pred_last
                for bb in bbs:
                    bkF, bkG = 2, 3
                    W("tensor", gws, wg_ld)
                    W("tensor", "v", xnT8_ready[bb])
                    bank_wait("tensor", bkF)
                    bank_wait("tensor", bkG)
                    for g in range(G):
                        opn("tensor", lambda e, g=g, bi=bufi, bb=bb, bk=bkF:
                            e.matmul(pb[bk][:, :], wg0st[bi][:, g, :],
                                     xnT8[:, g, bb, :], start=(g == 0),
                                     stop=(g == G - 1)))
                    for g in range(G):
                        bld = (lambda e, g=g, bi=bufi, bb=bb, bk=bkG: e.matmul(
                            pb[bk][:, :], wg1st[bi][:, g, :],
                            xnT8[:, g, bb, :], start=(g == 0),
                            stop=(g == G - 1)))
                        if g < G - 1:
                            opn("tensor", bld)
                        else:
                            t_mm = op("tensor", "t", bld)
                    a8b = ffn_state["a8b"]
                    W("scalar", "t", t_mm)
                    W("scalar", "g", act_prev[0])
                    t_efg = op("scalar", "s", lambda e, bk=bkF: e.activation(
                        ffg[:, :], pb[bk][:, :], AF.Identity))
                    W("vector", "t", t_mm)
                    W("vector", "g", act_prev[0])
                    evb = op("vector", "v", lambda e, bk=bkG: e.tensor_copy(
                        ff1[:, :], pb[bk][:, :]))
                    bank_guard[bkF] = ("s", t_efg)
                    bank_guard[bkG] = ("v", evb)
                    W("gpsimd", "s", t_efg)
                    opn("gpsimd", lambda e: e.tensor_mul(
                        gt[0][:, :], ffg[:, :], ffg[:, :]))
                    t_x3 = op("gpsimd", "g", lambda e: e.tensor_mul(
                        gt[1][:, :], gt[0][:, :], ffg[:, :]))
                    W("vector", "g", t_x3)
                    opn("vector", lambda e: e.tensor_scalar_mul(
                        gt[0][:, :], gt[1][:, :], C1))
                    opn("vector", lambda e: e.tensor_add(
                        gt[1][:, :], ffg[:, :], gt[0][:, :]))
                    t_g5 = op("vector", "v", lambda e: e.tensor_scalar_mul(
                        gt[0][:, :], gt[1][:, :], C2))
                    W("scalar", "v", t_g5)
                    opn("scalar", lambda e: e.activation(
                        gt[1][:, :], gt[0][:, :], AF.Tanh))
                    t_th = op("scalar", "s", lambda e: e.activation(
                        gt[0][:, :], gt[1][:, :], AF.Identity, scale=0.5,
                        bias=halfb[:, 0:1]))
                    th_prev[0] = t_th
                    W("gpsimd", "s", t_th)
                    opn("gpsimd", lambda e: e.tensor_mul(
                        gt[1][:, :], ffg[:, :], gt[0][:, :]))
                    t_act = op("gpsimd", "g", lambda e: e.tensor_mul(
                        ga[:, :], gt[1][:, :], ff1[:, :]))
                    act_prev[0] = t_act
                    W("scalar", "g", t_act)
                    W("scalar", "gs%d" % a8b, a8_guard[a8b])
                    t_a8 = op("scalar", "s", lambda e, a=a8b: e.activation(
                        act8[a][:, :], ga[:, :], AF.Identity))
                    W("gpsimd", "s", t_a8)
                    a8_guard[a8b] = dma("gpsimd", "gs%d" % a8b,
                                        act_d[bb, j * 128:(j + 1) * 128, :],
                                        act8[a8b][:, :])
                    ffn_state["a8b"] ^= 1
                    emitted += 1
                assert cnt["t"] == pred_last
                ffn_nxt[j] = upto_b + 1
            return emitted

        wo_mm_last = 0
        tr_last_t = 0
        xn_cnt_hist = [0, 0]
        at_guard = {0: 0}
        st = {"tr_last_t": 0, "t_sq_last": 0, "wo_mm_last": 0,
              "t_rrwkv_prev": 0}

        def emit_prep(b):
            for ti, r0 in enumerate(R0S):
                buf = 0
                gv, gsc = xt_guard[buf]
                W("gpsimd", "v", gv)
                W("gpsimd", "s", gsc)
                lxs = "lx%d" % (ti % 2)
                t_x = dma("gpsimd", lxs, xtile[buf][:, :],
                          x_d[b, r0:r0 + 128, :])
                W("scalar", lxs, t_x)
                W("scalar", "v", xn_cnt_hist[ti % 2])
                pc = 16 if ti % 2 == 0 else 20
                accol = 1 if ti % 2 == 0 else 13
                sumcol = 0 if ti % 2 == 0 else 14
                for half in range(2):
                    opn("scalar", lambda e, xa=xtile[buf], h=half, c=pc:
                        e.activation(psB[:, 1024:2048],
                                     xa[:, h * 1024:(h + 1) * 1024],
                                     AF.Identity,
                                     accum_out=stats[:, c + h:c + h + 1]))
                for half in range(2):
                    t_sq = op("scalar", "s",
                              lambda e, xa=xtile[buf], h=half, c=pc:
                              e.activation(psB[:, 1024:2048],
                                           xa[:, h * 1024:(h + 1) * 1024],
                                           AF.Square,
                                           accum_out=stats[:, c + 2 + h:c + 3 + h]))
                st["t_sq_last"] = t_sq
                W("vector", "s", t_sq)
                opn("vector", lambda e, c=pc, sc=sumcol: e.tensor_add(
                    stats[:, sc:sc + 1], stats[:, c:c + 1], stats[:, c + 1:c + 2]))
                opn("vector", lambda e, c=pc, sc=accol: e.tensor_add(
                    stats[:, sc:sc + 1], stats[:, c + 2:c + 3],
                    stats[:, c + 3:c + 4]))
                opn("vector", lambda e, c=sumcol: e.tensor_scalar_mul(
                    stats[:, 2:3], stats[:, c:c + 1], 1.0 / H))
                opn("vector", lambda e, c=accol: e.tensor_scalar_mul(
                    stats[:, 3:4], stats[:, c:c + 1], 1.0 / H))
                opn("vector", lambda e: e.tensor_mul(
                    stats[:, 4:5], stats[:, 2:3], stats[:, 2:3]))
                opn("vector", lambda e: e.tensor_sub(
                    stats[:, 5:6], stats[:, 3:4], stats[:, 4:5]))
                opn("vector", lambda e: e.tensor_scalar_add(
                    stats[:, 6:7], stats[:, 5:6], 1e-5))
                opn("vector", lambda e: e.tensor_scalar(
                    stats[:, 7:8], stats[:, 6:7], -0.5, 1.5, ALU.mult, ALU.add))
                opn("vector", lambda e: e.tensor_scalar_max(
                    stats[:, 8:9], stats[:, 7:8], 0.25))
                cur, alt = 8, 12
                for _ in range(3):
                    opn("vector", lambda e, c=cur: e.tensor_mul(
                        stats[:, 9:10], stats[:, c:c + 1], stats[:, c:c + 1]))
                    opn("vector", lambda e: e.tensor_mul(
                        stats[:, 10:11], stats[:, 9:10], stats[:, 6:7]))
                    opn("vector", lambda e: e.tensor_scalar(
                        stats[:, 11:12], stats[:, 10:11], -0.5, 1.5,
                        ALU.mult, ALU.add))
                    opn("vector", lambda e, c=cur, a=alt: e.tensor_mul(
                        stats[:, a:a + 1], stats[:, c:c + 1], stats[:, 11:12]))
                    cur, alt = alt, cur
                t_xn = 0
                for hh in range(2):
                    W("vector", "t", st["tr_last_t"])
                    t_xn = op("vector", "v",
                              lambda e, xa=xtile[buf], c=cur, hh=hh:
                              e.tensor_scalar(
                                  xn_t[:, :],
                                  xa[:, hh * 1024:(hh + 1) * 1024],
                                  stats[:, 2:3], stats[:, c:c + 1],
                                  ALU.subtract, ALU.mult))
                    for gq in range(2):
                        bidx = gq % 2
                        bank_wait("tensor", bidx)
                        W("tensor", "v", t_xn)
                        for q in range(4):
                            g = gq * 4 + q
                            bld = (lambda e, g=g, q=q, bi=bidx: e.transpose(
                                pb[bi][:, q * 128:(q + 1) * 128],
                                xn_t[:, g * 128:(g + 1) * 128], ident_s[:, :]))
                            if q < 3:
                                opn("tensor", bld)
                            else:
                                t_tr = op("tensor", "t", bld)
                        st["tr_last_t"] = t_tr
                        W("vector", "t", t_tr)
                        gq8 = hh * 2 + gq
                        ev = op("vector", "v",
                                lambda e, gq8=gq8, bi=bidx, r0=r0:
                                e.tensor_copy(
                                    xnT[:, gq8 * 4:gq8 * 4 + 4, r0:r0 + 128],
                                    pb[bi].rearrange("p (i c) -> p i c", i=4)))
                        bank_guard[bidx] = ("v", ev)
                xt_guard[buf] = (t_xn, t_sq)
                xn_cnt_hist[ti % 2] = t_xn
            for bk in (6, 7):
                bank_guard[bk] = ("s", st["t_sq_last"])
            t_xnT8 = op("vector", "v", lambda e, b=b: e.tensor_copy(
                xnT8[:, :, b, :], xnT[:, :, 1:CT + 1]))
            xnT8_ready[b] = t_xnT8

        emit_prep(0)
        for b in range(B):
            proj_specs = ((tmk_s, omtk, wkb_d, wk_d, "gck"),
                          (tmv_s, omtv, wvb_d, wv_d, "gcv"),
                          (tmr_s, omtr, wrb_d, wr_d, "gcr"))
            t_prev_w_mm = wo_mm_last
            ek_done = tr_done = vt_done = 0
            for w in range(3):
                tm_sb, om_sb, W_dram, W_f32, csem = proj_specs[w]
                W("vector", "t", t_prev_w_mm)
                for g in range(G):
                    opn("vector", lambda e, g=g, tm=tm_sb: e.tensor_scalar(
                        s1[0][:, :], xnT[:, g, 1:CT + 1], tm[:, g:g + 1], None,
                        ALU.mult))
                    bld = (lambda e, g=g, om=om_sb: e.scalar_tensor_tensor(
                        mixT[:, g, :], xnT[:, g, 0:CT], om[:, g:g + 1],
                        s1[0][:, :], ALU.mult, ALU.add))
                    if g < G - 1:
                        opn("vector", bld)
                    else:
                        t_mix = op("vector", "v", bld)

                t_base = cnt["t"]
                pred = {h: t_base + h + 1 for h in range(G)}
                wld = []
                if b == 0:
                    W("sync", csem, cast_done[csem])
                for h in range(G):
                    bufi = h % 2
                    W("sync", "t", gw_guard[bufi])
                    gws = "gw%d" % bufi
                    wld.append((gws, dma("sync", gws, wst[bufi][:, :, :],
                                         W_dram[h, :, :, :])))
                    gw_guard[bufi] = pred[h]
                for h in range(G):
                    bufi = h % 2
                    bidx = w if h % 2 == 0 else w + 4
                    W("tensor", wld[h][0], wld[h][1])
                    if h == 0:
                        W("tensor", "v", t_mix)
                    bank_wait("tensor", bidx)
                    for g in range(G):
                        bld = (lambda e, g=g, bi=bufi, bk=bidx: e.matmul(
                            pb[bk][:, :], wst[bi][:, g, :], mixT[:, g, :],
                            start=(g == 0), stop=(g == G - 1)))
                        if g < G - 1:
                            opn("tensor", bld)
                        else:
                            t_mm = op("tensor", "t", bld)
                    assert t_mm == pred[h]
                    if w == 0:
                        W("scalar", "t", t_mm)
                        ev = op("scalar", "s", lambda e, h=h, bk=bidx:
                                e.activation(ek[:, h, :], pb[bk][:, :], AF.Exp))
                        bank_guard[bidx] = ("s", ev)
                        ek_done = ev
                    elif w == 1:
                        W("vector", "t", t_mm)
                        ev = op("vector", "v", lambda e, h=h, bk=bidx:
                                e.tensor_copy(vT[:, h, :], pb[bk][:, :]))
                        bank_guard[bidx] = ("v", ev)
                        vt_done = ev
                    else:
                        W("scalar", "t", t_mm)
                        ev = op("scalar", "s", lambda e, h=h, bk=bidx:
                                e.activation(tanhr[:, h, :], pb[bk][:, :],
                                             AF.Tanh, scale=0.5))
                        bank_guard[bidx] = ("s", ev)
                        tr_done = ev
                t_prev_w_mm = t_mm
            t_proj_mm_last = t_prev_w_mm

            W("vector", "s", ek_done)
            for g in range(G):
                wb = wdec[:, g:g + 1].broadcast_to([128, CT])
                opn("vector", lambda e, g=g: e.tensor_mul(
                    ekv_s[:, :], ek[:, g, :], vT[:, g, :]))
                opn("vector", lambda e, wb=wb: e.tensor_tensor_scan(
                    atr[:, 1:CT + 1], wb, ekv_s[:, :], 0.0, ALU.mult, ALU.add))
                opn("vector", lambda e, g=g: e.tensor_copy(
                    afbf[:, g:g + 1], atr[:, CT:CT + 1]))
                opn("vector", lambda e, g=g, wb=wb: e.tensor_tensor_scan(
                    btr[:, 1:CT + 1], wb, ek[:, g, :], 0.0, ALU.mult, ALU.add))
                bld = (lambda e, g=g: e.tensor_copy(
                    afbf[:, G + g:G + g + 1], btr[:, CT:CT + 1]))
                if g < G - 1:
                    opn("vector", bld)
                else:
                    t_scan = op("vector", "v", bld)

            W("gpsimd", "v", t_scan)
            t_agin = dma("gpsimd", "gmisc", agin[b][:, :], afbf[:, :])
            W("gpsimd", "gmisc", t_agin)
            cnt["cc"] += 1
            emit("gpsimd", lambda e, b=b: e.collective_compute(
                "AllGather", ALU.bypass, replica_groups=[list(range(NCORE))],
                ins=[agin[b].ap().opt()], outs=[agout[b].ap().opt()],
            ).then_inc(sems["cc"], 1))
            t_cc = cnt["cc"]
            if b == 0:
                emit_cast("gcl", wlb_d, wl_d)
            ffn12_drain(b, max_units=40)
            if b + 1 < B:
                emit_prep(b + 1)

            W("gpsimd", "cc", t_cc)
            gre = agout[b].ap().rearrange("(m p) c -> p m c", p=128)
            dma("gpsimd", "gmisc", g_sb[:, 0:4, :], gre[:, 0:4, :])
            t_G = dma("gpsimd", "gmisc", g_sb[:, 4:8, :], gre[:, 4:8, :])

            W("vector", "gmisc", t_G)
            W("vector", "s", t_wm)
            for q in range(2):
                opn("vector", lambda e, q=q: e.tensor_mul(
                    acca[:, :], g_sb[:, 0, q * G:(q + 1) * G], wm_s[:, 0:G]))
                accs = [acca, accb]
                src = acca
                for m in range(1, 8):
                    opn("vector", lambda e, q=q, m=m: e.tensor_mul(
                        tmpm[:, :], g_sb[:, m, q * G:(q + 1) * G],
                        wm_s[:, m * G:(m + 1) * G]))
                    if m == 7:
                        opn("vector", lambda e, q=q, sc=src: e.tensor_add(
                            sab[:, q * G:(q + 1) * G], sc[:, :], tmpm[:, :]))
                    else:
                        dst = accs[m % 2]
                        opn("vector", lambda e, d=dst, sc=src: e.tensor_add(
                            d[:, :], sc[:, :], tmpm[:, :]))
                        src = dst
            W("vector", "t", t_proj_mm_last)
            t_rec_by_g = {}
            den_by_g = {-1: 0}

            def out_phase1(g):
                wb = wdec[:, g:g + 1].broadcast_to([128, CT])
                par = g % 2
                W("gpsimd", "v", den_by_g[g - 1])
                t_ekv = op("gpsimd", "g", lambda e, g=g: e.tensor_mul(
                    ekv_s[:, :], ek[:, g, :], vT[:, g, :]))
                W("vector", "g", t_ekv)
                opn("vector", lambda e, g=g, wb=wb: e.tensor_tensor_scan(
                    atr[:, 1:CT + 1], wb, ekv_s[:, :], sab[:, g:g + 1],
                    ALU.mult, ALU.add))
                opn("vector", lambda e, g=g: e.tensor_copy(
                    atr[:, 0:1], sab[:, g:g + 1]))
                opn("vector", lambda e, g=g, wb=wb: e.tensor_tensor_scan(
                    btr[:, 1:CT + 1], wb, ek[:, g, :],
                    sab[:, G + g:G + g + 1], ALU.mult, ALU.add))
                opn("vector", lambda e, g=g: e.tensor_copy(
                    btr[:, 0:1], sab[:, G + g:G + g + 1]))
                num_by_g[g] = op("vector", "v", lambda e, g=g, p=par:
                                 e.scalar_tensor_tensor(
                    s1[p][:, :], ekv_s[:, :], etf_s[:, g:g + 1], atr[:, 0:CT],
                    ALU.mult, ALU.add))
                t_den = op("vector", "v", lambda e, g=g, p=par:
                           e.scalar_tensor_tensor(
                    s2[p][:, :], ek[:, g, :], etf_s[:, g:g + 1], btr[:, 0:CT],
                    ALU.mult, ALU.add))
                den_by_g[g] = t_den
                W("scalar", "v", t_den)
                opn("scalar", lambda e, p=par: e.activation(
                    s3[p][:, :], s2[p][:, :], AF.Ln))
                t_rec_by_g[g] = op("scalar", "s", lambda e, p=par: e.activation(
                    s4[p][:, :], s3[p][:, :], AF.Exp, bias=nhb[:, 0:1],
                    scale=-1.0))

            num_by_g = {}

            rr_prev = [0]

            def out_phase2(g, last):
                par = g % 2
                W("gpsimd", "s", t_rec_by_g[g])
                W("gpsimd", "v", num_by_g[g])
                W("gpsimd", "v", rr_prev[0])
                t_wkv = op("gpsimd", "g", lambda e, p=par: e.tensor_mul(
                    s5[:, :], s1[p][:, :], s4[p][:, :]))
                W("vector", "g", t_wkv)
                t_rr = op("vector", "v", lambda e, g=g: e.scalar_tensor_tensor(
                    mixT[:, g, :], tanhr[:, g, :], 1.0, s5[:, :],
                    ALU.add, ALU.mult))
                rr_prev[0] = t_rr
                return t_rr

            rr_by_g = {}
            out_phase1(0)
            for g in range(1, G):
                out_phase1(g)
                rr_by_g[g - 1] = out_phase2(g - 1, False)
            rr_by_g[G - 1] = t_rrwkv = out_phase2(G - 1, True)
            st["t_rrwkv_prev"] = t_rrwkv

            banks = [0, 1, 4, 5]
            for ob in range(4):
                t_base = cnt["t"]
                pred = {h: t_base + h + 1 for h in range(G)}
                woldc = []
                if b == 0 and ob == 0:
                    W("sync", "gco", cast_done["gco"])
                for h in range(G):
                    bufi = h % 2
                    W("sync", "t", gw_guard[bufi])
                    gws = "gw%d" % bufi
                    woldc.append((gws, dma(
                        "sync", gws,
                        wst[bufi].ap().rearrange("p g c -> p (g c)")[:, 0:512],
                        wob_d[h, ob, :, :])))
                    gw_guard[bufi] = pred[h]
                for h in range(G):
                    bufi = h % 2
                    W("tensor", woldc[h][0], woldc[h][1])
                    W("tensor", "v", rr_by_g[h])
                    if h == 0:
                        for bk in banks:
                            bank_wait("tensor", bk)
                    for tsub in range(4):
                        bld = (lambda e, h=h, ts=tsub, bi=bufi, bk=banks[tsub]:
                               e.matmul(
                                   pb[bk][:, :],
                                   mixT[:, h, ts * 128:(ts + 1) * 128],
                                   wst[bi].ap().rearrange(
                                       "p g c -> p (g c)")[:, 0:512],
                                   start=(h == 0), stop=(h == G - 1)))
                        if tsub < 3:
                            opn("tensor", bld)
                        else:
                            t_mm = op("tensor", "t", bld)
                    assert t_mm == pred[h]
                for tsub in range(4):
                    ai = tsub % 2
                    W("vector", "t", t_mm)
                    W("vector", "sa" if ai == 0 else "sa1",
                      at_guard.get(ai, 0))
                    tev = op("vector", "v", lambda e, bk=banks[tsub], ai=ai:
                             e.tensor_copy(attw[ai][:, :], pb[bk][:, :]))
                    bank_guard[banks[tsub]] = ("v", tev)
                    W("gpsimd", "v", tev)
                    at_guard[ai] = dma("gpsimd", "sa" if ai == 0 else "sa1",
                                       attn_d[b, tsub * 128:(tsub + 1) * 128,
                                              ob * 512:(ob + 1) * 512],
                                       attw[ai][:, :])
                wo_mm_last = t_mm
            ffn12_drain(b)

        ffn12_drain(B - 1)
        attn_ctx.close()

        ffn_ctx = ExitStack()
        fa = ffn_ctx.enter_context
        wlsb = fa(nc.sbuf_tensor("wlsb", [128, NJ, 1024], F8))
        actsb = [fa(nc.sbuf_tensor("actsb%d" % i, [128, NJ, 512], F8))
                 for i in range(2)]
        f1t = [fa(nc.sbuf_tensor("f1t%d" % i, [128, 512], BF16))
               for i in range(2)]
        xrt = [fa(nc.sbuf_tensor("xrt%d" % i, [128, 512], F32))
               for i in range(2)]
        att2 = [fa(nc.sbuf_tensor("att2_%d" % i, [128, 512], BF16))
                for i in range(2)]
        o1t = [fa(nc.sbuf_tensor("o1t%d" % i, [128, 512], F32))
               for i in range(2)]
        outt = [fa(nc.sbuf_tensor("outt%d" % i, [128, 512], F32))
                for i in range(2)]

        t_act_s0, t_act_s1 = cnt["gs0"], cnt["gs1"]
        t_attn_sa = cnt["sa"]
        t_attn_sa1 = cnt["sa1"]

        t_ffn3_mm = 0
        ev_guard = {}
        ob2 = 0
        out_guard = {0: 0, 1: 0}
        xr_guard = {0: 0, 1: 0}
        W("sync", "sa", t_attn_sa)
        W("sync", "sa1", t_attn_sa1)
        W("sync", "gcl", cast_done["gcl"])
        DR = mybir.MatmulPerfMode.DoubleRow
        act_ld_cnt = {0: 0, 1: 0}
        mm_done = {}
        for obh in range(2):
            W("sync", "t", t_ffn3_mm)
            wl_ld = 0
            for j in range(NJ):
                wl_ld = dma("sync", "gwl", wlsb[:, j, :],
                            wlb_d[j * 128:(j + 1) * 128,
                                  obh * 1024:(obh + 1) * 1024])
            for bb in range(B):
                w = obh * B + bb
                par = w % 2
                if act_ld_cnt[par] == 0 or True:
                    W("gpsimd", "gs0", t_act_s0)
                    W("gpsimd", "gs1", t_act_s1)
                    W("gpsimd", "t", mm_done.get(w - 2, 0))
                    lsem = "lact" if par == 0 else "lact1"
                    for j0 in range(0, NJ, 8):
                        act_ld_cnt[par] = dma(
                            "gpsimd", lsem, actsb[par][:, j0:j0 + 8, :],
                            act_d[bb, j0 * 128:(j0 + 8) * 128, :]
                            .rearrange("(j p) t -> p j t", p=128))
                W("tensor", "gwl", wl_ld)
                if obh == 0 and bb == 0:
                    for bk in range(8):
                        bank_wait("tensor", bk)
                W("tensor", "lact" if par == 0 else "lact1", act_ld_cnt[par])
                t_half = {}
                for half in range(2):
                    tsubs = (0, 1) if half == 0 else (2, 3)
                    for ts_ in tsubs:
                        for ot in range(2):
                            W("tensor", "v", ev_guard.get(ts_ * 2 + ot, 0))
                    for jp in range(NJ // 2):
                        for ts_ in tsubs:
                            for ot in range(2):
                                bld = (lambda e, jp=jp, ts=ts_, ot=ot, p2=par:
                                       e.matmul(
                                    pb[ts * 2 + ot][:, :],
                                    actsb[p2][:, 2 * jp:2 * jp + 2,
                                              ts * 128:(ts + 1) * 128],
                                    wlsb[:, 2 * jp:2 * jp + 2,
                                         ot * 512:(ot + 1) * 512],
                                    start=(jp == 0),
                                    stop=(jp == NJ // 2 - 1),
                                    perf_mode=DR))
                                if (jp == NJ // 2 - 1 and ts_ == tsubs[-1]
                                        and ot == 1):
                                    t_half[half] = op("tensor", "t", bld)
                                else:
                                    opn("tensor", bld)
                t_ffn3_mm = t_half[1]
                mm_done[w] = t_ffn3_mm
                for tsub in range(4):
                    for ot in range(2):
                        col0 = obh * 1024 + ot * 512
                        W("vector", "t", t_half[0 if tsub < 2 else 1])
                        tfv = op("vector", "v", lambda e, bk=tsub * 2 + ot,
                                 c=ob2: e.tensor_copy(f1t[c][:, :],
                                                      pb[bk][:, :]))
                        ev_guard[tsub * 2 + ot] = tfv
                        W("sync", "v", xr_guard[ob2])
                        lfs = "lf%d" % ob2
                        dma("sync", lfs, xrt[ob2][:, :],
                            x_d[bb, 1 + tsub * 128:1 + (tsub + 1) * 128,
                                col0:col0 + 512])
                        t_al = dma("sync", lfs, att2[ob2][:, :],
                                   attn_d[bb, tsub * 128:(tsub + 1) * 128,
                                          col0:col0 + 512])
                        W("vector", lfs, t_al)
                        W("vector", "so%d" % ob2, out_guard[ob2])
                        opn("vector", lambda e, c=ob2: e.tensor_add(
                            o1t[c][:, :], xrt[c][:, :], att2[c][:, :]))
                        t_ov = op("vector", "v", lambda e, c=ob2: e.tensor_add(
                            outt[c][:, :], o1t[c][:, :], f1t[c][:, :]))
                        xr_guard[ob2] = t_ov
                        W("sync", "v", t_ov)
                        out_guard[ob2] = dma(
                            "sync", "so%d" % ob2,
                            out_d[bb, tsub * 128:(tsub + 1) * 128,
                                  col0:col0 + 512],
                            outt[ob2][:, :])
                        ob2 ^= 1

        ffn_ctx.close()

        W("sync", "so0", cnt["so0"])
        W("sync", "so1", cnt["so1"])
        W("gpsimd", "gs0", cnt["gs0"])
        W("gpsimd", "gs1", cnt["gs1"])
        W("gpsimd", "sa", cnt["sa"])
        W("gpsimd", "sa1", cnt["sa1"])

        block = ctx.enter_context(nc.Block())

        @block.sync
        def _(e):
            for f in steps["sync"]:
                f(e)

        @block.gpsimd
        def _(e):
            for f in steps["gpsimd"]:
                f(e)

        @block.vector
        def _(e):
            for f in steps["vector"]:
                f(e)

        @block.scalar
        def _(e):
            for f in steps["scalar"]:
                f(e)

        @block.tensor
        def _(e):
            for f in steps["tensor"]:
                f(e)

    return nc


_NC_CACHE = []


def kernel(L0_attn_hidden, L0_attn_time_decay, L0_attn_time_first,
           L0_attn_time_mix_key, L0_attn_time_mix_value,
           L0_attn_time_mix_receptance, L0_attn_Wk, L0_attn_Wv, L0_attn_Wr,
           L0_attn_Wo, L0_ffn_w_gating, L0_ffn_w_linear):
    x = np.ascontiguousarray(np.asarray(L0_attn_hidden, dtype=np.float32))
    x_pad = np.concatenate([np.zeros((B, 1, H), np.float32), x], axis=1)

    def vec16(v):
        return np.ascontiguousarray(
            np.asarray(v, np.float32).reshape(-1).reshape(G, 128).T)

    def tile_proj(w):
        return np.ascontiguousarray(
            np.asarray(w, np.float32).reshape(G, 128, G, 128)
            .transpose(2, 1, 0, 3))

    wk = tile_proj(L0_attn_Wk)
    wv = tile_proj(L0_attn_Wv)
    wr = tile_proj(L0_attn_Wr)
    wo = np.ascontiguousarray(
        np.asarray(L0_attn_Wo, np.float32).reshape(G, 128, 4, 512)
        .transpose(0, 2, 1, 3))
    wg = np.asarray(L0_ffn_w_gating, np.float32)

    def tile_ffn(w):
        return np.ascontiguousarray(
            w.reshape(G, 128, NJ, 128).transpose(2, 1, 0, 3))

    wg0 = tile_ffn(wg[0])
    wg1 = tile_ffn(wg[1])
    wl = np.ascontiguousarray(np.asarray(L0_ffn_w_linear, np.float32))
    tmk = vec16(L0_attn_time_mix_key)
    tmv = vec16(L0_attn_time_mix_value)
    tmr = vec16(L0_attn_time_mix_receptance)
    td = vec16(L0_attn_time_decay)
    tf = vec16(L0_attn_time_first)
    ident = np.eye(128, dtype=np.float32)

    if not _NC_CACHE:
        _NC_CACHE.append(build_program())
    nc = _NC_CACHE[0]

    in_maps = []
    for c in range(NCORE):
        row = np.array([512.0 * (c - 1 - m) if m < c else 1e9
                        for m in range(8)], np.float32)
        ecnt = np.ascontiguousarray(np.broadcast_to(row, (128, 8)))
        xs = np.ascontiguousarray(x_pad[:, c * CT:c * CT + CT + 1, :])
        in_maps.append({
            "x": xs, "wk": wk, "wv": wv, "wr": wr, "wo": wo,
            "wg0": wg0, "wg1": wg1, "wl": wl,
            "tmk": tmk, "tmv": tmv, "tmr": tmr, "td": td, "tf": tf,
            "ecnt": ecnt, "ident": ident,
        })

    res = run_bass_kernel_spmd(nc, in_maps, core_ids=list(range(NCORE)))
    out = np.empty((B, T, H), np.float32)
    for c in range(NCORE):
        out[:, c * CT:(c + 1) * CT, :] = res.results[c]["out"]
    return out


# version 21
# speedup vs baseline: 1.1500x; 1.0151x over previous
import numpy as np
import ml_dtypes
from contextlib import ExitStack

import concourse.bass as bass
import concourse.mybir as mybir
from concourse.bass_utils import run_bass_kernel_spmd

F32 = mybir.dt.float32
BF16 = mybir.dt.bfloat16
F8 = mybir.dt.float8e4
ALU = mybir.AluOpType
AF = mybir.ActivationFunctionType
AX = mybir.AxisListType

B, T, H = 4, 4096, 2048
NCORE = 8
CT = T // NCORE
G = H // 128
JH = 8192
NJ = JH // 128

C1 = float(np.float32(ml_dtypes.bfloat16(0.044715)))
C2 = float(np.float32(ml_dtypes.bfloat16(np.sqrt(2.0 / np.pi))))
NEG_LN2 = -0.6931471805599453

ENGS = ("sync", "gpsimd", "vector", "scalar", "tensor")


def build_program():
    nc = bass.Bass("TRN2", target_bir_lowering=False)

    x_d = nc.dram_tensor("x", [B, CT + 1, H], F32, kind="ExternalInput")
    wk_d = nc.dram_tensor("wk", [G, 128, G, 128], F32, kind="ExternalInput")
    wv_d = nc.dram_tensor("wv", [G, 128, G, 128], F32, kind="ExternalInput")
    wr_d = nc.dram_tensor("wr", [G, 128, G, 128], F32, kind="ExternalInput")
    wo_d = nc.dram_tensor("wo", [G, 4, 128, 512], F32, kind="ExternalInput")
    wg0_d = nc.dram_tensor("wg0", [NJ, 128, G, 128], F32, kind="ExternalInput")
    wg1_d = nc.dram_tensor("wg1", [NJ, 128, G, 128], F32, kind="ExternalInput")
    wl_d = nc.dram_tensor("wl", [JH, H], F32, kind="ExternalInput")
    tmk_d = nc.dram_tensor("tmk", [128, G], F32, kind="ExternalInput")
    tmv_d = nc.dram_tensor("tmv", [128, G], F32, kind="ExternalInput")
    tmr_d = nc.dram_tensor("tmr", [128, G], F32, kind="ExternalInput")
    td_d = nc.dram_tensor("td", [128, G], F32, kind="ExternalInput")
    tf_d = nc.dram_tensor("tf", [128, G], F32, kind="ExternalInput")
    ecnt_d = nc.dram_tensor("ecnt", [128, 8], F32, kind="ExternalInput")
    ident_d = nc.dram_tensor("ident", [128, 128], F32, kind="ExternalInput")
    out_d = nc.dram_tensor("out", [B, CT, H], F32, kind="ExternalOutput")

    wkb_d = nc.dram_tensor("wkb", [G, 128, G, 128], BF16)
    wvb_d = nc.dram_tensor("wvb", [G, 128, G, 128], BF16)
    wrb_d = nc.dram_tensor("wrb", [G, 128, G, 128], BF16)
    wob_d = nc.dram_tensor("wob", [G, 4, 128, 512], BF16)
    wg0b_d = nc.dram_tensor("wg0b", [NJ, 128, G, 128], F8)
    wg1b_d = nc.dram_tensor("wg1b", [NJ, 128, G, 128], F8)
    wlb_d = nc.dram_tensor("wlb", [JH, H], F8)
    attn_d = nc.dram_tensor("attn_sc", [B, CT, H], BF16)
    act_d = nc.dram_tensor("act_sc", [B, JH, CT], F8)
    agin = [nc.dram_tensor("agin%d" % b, [128, 32], F32) for b in range(B)]
    agout = [nc.dram_tensor("agout%d" % b, [1024, 32], F32) for b in range(B)]

    steps = {k: [] for k in ENGS}
    cnt = dict(lmisc=0, lx0=0, lx1=0, lact=0, lact1=0, lf0=0, lf1=0,
               so0=0, so1=0, gw0=0, gw1=0, gwl=0, gmisc=0,
               gs0=0, gs1=0, sa=0, sa1=0, v=0, s=0, t=0, cc=0, g=0,
               gck=0, gcv=0, gcr=0, gco=0, gc0=0, gc1=0, gcl=0)
    sems = {}

    def emit(eng, f):
        steps[eng].append(f)

    def W(eng, sem, val):
        if val <= 0:
            return
        emit(eng, lambda e, sname=sem, v=val: e.wait_ge(sems[sname], v))

    def dma(eng, sem, out_ap, in_ap):
        cnt[sem] += 16
        emit(eng, lambda e, o=out_ap, i=in_ap, sname=sem:
             e.dma_start(out=o, in_=i).then_inc(sems[sname], 16))
        return cnt[sem]

    _OWN = {"vector": "v", "scalar": "s", "gpsimd": "g"}

    def op(eng, sem, bld):
        own = _OWN.get(eng)
        if own is not None:
            assert sem == own
            W(eng, own, cnt[own])
        cnt[sem] += 1
        emit(eng, lambda e, bd=bld, sname=sem: bd(e).then_inc(sems[sname], 1))
        return cnt[sem]

    def opn(eng, bld):
        own = _OWN.get(eng)
        if own is not None:
            return op(eng, own, bld)
        emit(eng, lambda e, bd=bld: bd(e))

    def mark(eng, sem):
        cnt[sem] += 1
        emit(eng, lambda e, sname=sem: e.nop().then_inc(sems[sname], 1))
        return cnt[sem]

    ctx = ExitStack()
    with ctx:
        for name in ("lmisc", "lx0", "lx1", "lact", "lact1", "lf0", "lf1",
                     "so0", "so1", "gw0", "gw1", "gwl", "gmisc",
                     "gs0", "gs1", "sa", "sa1", "v", "s", "t", "cc", "g",
                     "gck", "gcv", "gcr", "gco", "gc0", "gc1", "gcl"):
            sems[name] = ctx.enter_context(nc.semaphore("sem_" + name))

        al = ctx.enter_context

        xnT8 = al(nc.sbuf_tensor("xnT8", [128, G, B, CT], F8))
        tmk_s = al(nc.sbuf_tensor("tmk_s", [128, G], F32))
        tmv_s = al(nc.sbuf_tensor("tmv_s", [128, G], F32))
        tmr_s = al(nc.sbuf_tensor("tmr_s", [128, G], F32))
        omtk = al(nc.sbuf_tensor("omtk", [128, G], F32))
        omtv = al(nc.sbuf_tensor("omtv", [128, G], F32))
        omtr = al(nc.sbuf_tensor("omtr", [128, G], F32))
        td_s = al(nc.sbuf_tensor("td_s", [128, G], F32))
        tf_s = al(nc.sbuf_tensor("tf_s", [128, G], F32))
        etd_s = al(nc.sbuf_tensor("etd_s", [128, G], F32))
        ntd_s = al(nc.sbuf_tensor("ntd_s", [128, G], F32))
        wdec = al(nc.sbuf_tensor("wdec", [128, G], F32))
        etf_s = al(nc.sbuf_tensor("etf_s", [128, G], F32))
        ecnt_s = al(nc.sbuf_tensor("ecnt_s", [128, 8], F32))
        ident_s = al(nc.sbuf_tensor("ident_s", [128, 128], F32))
        wmt = al(nc.sbuf_tensor("wmt", [128, 8 * G], F32))
        wm_s = al(nc.sbuf_tensor("wm_s", [128, 8 * G], F32))
        nhb = al(nc.sbuf_tensor("nhb", [128, 1], F32))
        halfb = al(nc.sbuf_tensor("halfb", [128, 1], F32))
        stats = al(nc.sbuf_tensor("stats", [128, 24], F32))
        afbf = al(nc.sbuf_tensor("afbf", [128, 32], F32))
        g_sb = al(nc.sbuf_tensor("g_sb", [128, 8, 32], F32))
        sab = al(nc.sbuf_tensor("sab", [128, 32], F32))
        acca = al(nc.sbuf_tensor("acca", [128, G], F32))
        accb = al(nc.sbuf_tensor("accb", [128, G], F32))
        tmpm = al(nc.sbuf_tensor("tmpm", [128, G], F32))

        wg0st = [al(nc.sbuf_tensor("wg0st%d" % i, [128, G, 128], F8))
                 for i in range(2)]
        wg1st = [al(nc.sbuf_tensor("wg1st%d" % i, [128, G, 128], F8))
                 for i in range(2)]
        ffg = al(nc.sbuf_tensor("ffg", [128, 512], BF16))
        ff1 = al(nc.sbuf_tensor("ff1", [128, 512], BF16))
        gt = [al(nc.sbuf_tensor("gt%d" % i, [128, 512], BF16)) for i in range(2)]
        act8 = [al(nc.sbuf_tensor("act8_%d" % i, [128, 512], F8))
                for i in range(2)]
        ga = al(nc.sbuf_tensor("ga", [128, 512], BF16))

        psA = al(nc.psum_tensor("psA", [128, 2048], F32))
        psB = al(nc.psum_tensor("psB", [128, 2048], F32))
        pb = [psA[:, i * 512:(i + 1) * 512] for i in range(4)] + \
             [psB[:, i * 512:(i + 1) * 512] for i in range(4)]

        for sb, dd in ((tmk_s, tmk_d), (tmv_s, tmv_d), (tmr_s, tmr_d),
                       (td_s, td_d), (tf_s, tf_d), (ecnt_s, ecnt_d),
                       (ident_s, ident_d)):
            dma("sync", "lmisc", sb[:, :], dd[:, :])
        t_par = cnt["lmisc"]

        cast_done = {}
        cast_state = {"prev": None}

        def emit_cast(sem, dst, srcw):
            prev = cast_state["prev"]
            if prev is not None:
                W("gpsimd", prev, cast_done[prev])
            step = dst.shape[0] // 4
            for i in range(4):
                cast_done[sem] = dma("gpsimd", sem,
                                     dst[i * step:(i + 1) * step],
                                     srcw[i * step:(i + 1) * step])
            cast_state["prev"] = sem

        emit_cast("gc0", wg0b_d, wg0_d)
        emit_cast("gc1", wg1b_d, wg1_d)
        emit_cast("gco", wob_d, wo_d)
        emit_cast("gck", wkb_d, wk_d)
        emit_cast("gcv", wvb_d, wv_d)
        emit_cast("gcr", wrb_d, wr_d)

        W("vector", "lmisc", t_par)
        opn("vector", lambda e: e.memset(nhb[:, :], NEG_LN2))
        opn("vector", lambda e: e.memset(halfb[:, :], 0.5))
        for tm, om in ((tmk_s, omtk), (tmv_s, omtv), (tmr_s, omtr)):
            opn("vector", lambda e, tm=tm, om=om:
                e.tensor_scalar(om[:, :], tm[:, :], -1.0, 1.0, ALU.mult, ALU.add))
        W("scalar", "lmisc", t_par)
        t_etd = op("scalar", "s", lambda e: e.activation(etd_s[:, :], td_s[:, :],
                                                         AF.Exp))
        W("vector", "s", t_etd)
        t_ntd = op("vector", "v", lambda e:
                   e.tensor_scalar_mul(ntd_s[:, :], etd_s[:, :], -1.0))
        W("scalar", "v", t_ntd)
        opn("scalar", lambda e: e.activation(wdec[:, :], ntd_s[:, :], AF.Exp))
        t_wdec = op("scalar", "s", lambda e: e.activation(etf_s[:, :], tf_s[:, :],
                                                          AF.Exp))
        W("vector", "s", t_wdec)
        for m in range(8):
            bld = (lambda e, m=m: e.tensor_scalar(
                wmt[:, m * G:(m + 1) * G], ntd_s[:, :], ecnt_s[:, m:m + 1],
                None, ALU.mult))
            if m < 7:
                opn("vector", bld)
            else:
                t_wmt = op("vector", "v", bld)
        W("scalar", "v", t_wmt)
        t_wm = op("scalar", "s", lambda e: e.activation(wm_s[:, :], wmt[:, :],
                                                        AF.Exp))

        attn_ctx = ExitStack()
        aa = attn_ctx.enter_context
        xnT = aa(nc.sbuf_tensor("xnT", [128, G, CT + 1], F32))
        mixT = aa(nc.sbuf_tensor("mixT", [128, G, CT], BF16))
        ek = aa(nc.sbuf_tensor("ek", [128, G, CT], F32))
        vT = aa(nc.sbuf_tensor("vT", [128, G, CT], BF16))
        tanhr = aa(nc.sbuf_tensor("tanhr", [128, G, CT], BF16))
        wst = [aa(nc.sbuf_tensor("wst%d" % i, [128, G, 128], BF16))
               for i in range(2)]
        xtile = [aa(nc.sbuf_tensor("xtile0", [128, H], F32))]
        xn_t = aa(nc.sbuf_tensor("xn_t", [128, H // 2], F32))
        atr = aa(nc.sbuf_tensor("atr", [128, CT + 1], F32))
        btr = aa(nc.sbuf_tensor("btr", [128, CT + 1], F32))
        s1 = [aa(nc.sbuf_tensor("s1_%d" % i, [128, CT], F32)) for i in range(2)]
        s5 = aa(nc.sbuf_tensor("s5", [128, CT], F32))
        s2 = [aa(nc.sbuf_tensor("s2_%d" % i, [128, CT], F32)) for i in range(2)]
        s3 = [aa(nc.sbuf_tensor("s3_0", [128, CT], F32))] * 2
        s4 = [aa(nc.sbuf_tensor("s4_%d" % i, [128, CT], F32)) for i in range(2)]
        ekv_s = aa(nc.sbuf_tensor("ekv_s", [128, CT], F32))
        attw = [aa(nc.sbuf_tensor("attw%d" % i, [128, 512], BF16))
                for i in range(2)]

        xt_guard = {0: (0, 0)}
        gw_guard = {0: 0, 1: 0}
        bank_guard = {}
        xn_hist = {0: 0, 1: 0, 2: 0}

        def bank_wait(eng, idx):
            if idx in bank_guard:
                sname, v = bank_guard[idx]
                W(eng, sname, v)

        R0S = [0, 128, 256, 384, 385]

        ffn_state = dict(gate=False, a8b=0)
        ffn_nxt = [0] * NJ
        a8_guard = {0: 0, 1: 0}
        th_prev = [0]
        act_prev = [0]
        xnT8_ready = {}

        def ffn12_drain(upto_b, max_units=None):
            emitted = 0
            for j in range(NJ):
                if ffn_nxt[j] > upto_b:
                    continue
                if max_units is not None and emitted >= max_units:
                    break
                bbs = list(range(ffn_nxt[j], upto_b + 1))
                if not ffn_state["gate"]:
                    W("sync", "gc0", cast_done["gc0"])
                    W("sync", "gc1", cast_done["gc1"])
                    ffn_state["gate"] = True
                bufi = j % 2
                t_base = cnt["t"]
                pred_last = t_base + len(bbs)
                W("sync", "t", gw_guard[bufi])
                gws = "gw%d" % bufi
                dma("sync", gws, wg0st[bufi][:, :, :], wg0b_d[j, :, :, :])
                wg_ld = dma("sync", gws, wg1st[bufi][:, :, :],
                            wg1b_d[j, :, :, :])
                gw_guard[bufi] = pred_last
                for bb in bbs:
                    bkF, bkG = 2, 3
                    W("tensor", gws, wg_ld)
                    W("tensor", "v", xnT8_ready[bb])
                    bank_wait("tensor", bkF)
                    bank_wait("tensor", bkG)
                    for g in range(G):
                        opn("tensor", lambda e, g=g, bi=bufi, bb=bb, bk=bkF:
                            e.matmul(pb[bk][:, :], wg0st[bi][:, g, :],
                                     xnT8[:, g, bb, :], start=(g == 0),
                                     stop=(g == G - 1)))
                    for g in range(G):
                        bld = (lambda e, g=g, bi=bufi, bb=bb, bk=bkG: e.matmul(
                            pb[bk][:, :], wg1st[bi][:, g, :],
                            xnT8[:, g, bb, :], start=(g == 0),
                            stop=(g == G - 1)))
                        if g < G - 1:
                            opn("tensor", bld)
                        else:
                            t_mm = op("tensor", "t", bld)
                    a8b = ffn_state["a8b"]
                    W("scalar", "t", t_mm)
                    W("scalar", "g", act_prev[0])
                    t_efg = op("scalar", "s", lambda e, bk=bkF: e.activation(
                        ffg[:, :], pb[bk][:, :], AF.Identity))
                    W("vector", "t", t_mm)
                    W("vector", "g", act_prev[0])
                    evb = op("vector", "v", lambda e, bk=bkG: e.tensor_copy(
                        ff1[:, :], pb[bk][:, :]))
                    bank_guard[bkF] = ("s", t_efg)
                    bank_guard[bkG] = ("v", evb)
                    W("gpsimd", "s", t_efg)
                    opn("gpsimd", lambda e: e.tensor_mul(
                        gt[0][:, :], ffg[:, :], ffg[:, :]))
                    t_x3 = op("gpsimd", "g", lambda e: e.tensor_mul(
                        gt[1][:, :], gt[0][:, :], ffg[:, :]))
                    W("vector", "g", t_x3)
                    opn("vector", lambda e: e.tensor_scalar_mul(
                        gt[0][:, :], gt[1][:, :], C1))
                    opn("vector", lambda e: e.tensor_add(
                        gt[1][:, :], ffg[:, :], gt[0][:, :]))
                    t_g5 = op("vector", "v", lambda e: e.tensor_scalar_mul(
                        gt[0][:, :], gt[1][:, :], C2))
                    W("scalar", "v", t_g5)
                    opn("scalar", lambda e: e.activation(
                        gt[1][:, :], gt[0][:, :], AF.Tanh))
                    t_th = op("scalar", "s", lambda e: e.activation(
                        gt[0][:, :], gt[1][:, :], AF.Identity, scale=0.5,
                        bias=halfb[:, 0:1]))
                    th_prev[0] = t_th
                    W("gpsimd", "s", t_th)
                    opn("gpsimd", lambda e: e.tensor_mul(
                        gt[1][:, :], ffg[:, :], gt[0][:, :]))
                    t_act = op("gpsimd", "g", lambda e: e.tensor_mul(
                        ga[:, :], gt[1][:, :], ff1[:, :]))
                    act_prev[0] = t_act
                    W("scalar", "g", t_act)
                    W("scalar", "gs%d" % a8b, a8_guard[a8b])
                    t_a8 = op("scalar", "s", lambda e, a=a8b: e.activation(
                        act8[a][:, :], ga[:, :], AF.Identity))
                    W("gpsimd", "s", t_a8)
                    a8_guard[a8b] = dma("gpsimd", "gs%d" % a8b,
                                        act_d[bb, j * 128:(j + 1) * 128, :],
                                        act8[a8b][:, :])
                    ffn_state["a8b"] ^= 1
                    emitted += 1
                assert cnt["t"] == pred_last
                ffn_nxt[j] = upto_b + 1
            return emitted

        wo_mm_last = 0
        tr_last_t = 0
        xn_cnt_hist = [0, 0]
        at_guard = {0: 0}
        st = {"tr_last_t": 0, "t_sq_last": 0, "wo_mm_last": 0,
              "t_rrwkv_prev": 0}

        def emit_prep(b):
            for ti, r0 in enumerate(R0S):
                buf = 0
                gv, gsc = xt_guard[buf]
                W("gpsimd", "v", gv)
                W("gpsimd", "s", gsc)
                lxs = "lx%d" % (ti % 2)
                t_x = dma("gpsimd", lxs, xtile[buf][:, :],
                          x_d[b, r0:r0 + 128, :])
                W("scalar", lxs, t_x)
                W("scalar", "v", xn_cnt_hist[ti % 2])
                pc = 16 if ti % 2 == 0 else 20
                accol = 1 if ti % 2 == 0 else 13
                sumcol = 0 if ti % 2 == 0 else 14
                for half in range(2):
                    opn("scalar", lambda e, xa=xtile[buf], h=half, c=pc:
                        e.activation(psB[:, 1024:2048],
                                     xa[:, h * 1024:(h + 1) * 1024],
                                     AF.Identity,
                                     accum_out=stats[:, c + h:c + h + 1]))
                for half in range(2):
                    t_sq = op("scalar", "s",
                              lambda e, xa=xtile[buf], h=half, c=pc:
                              e.activation(psB[:, 1024:2048],
                                           xa[:, h * 1024:(h + 1) * 1024],
                                           AF.Square,
                                           accum_out=stats[:, c + 2 + h:c + 3 + h]))
                st["t_sq_last"] = t_sq
                W("vector", "s", t_sq)
                opn("vector", lambda e, c=pc, sc=sumcol: e.tensor_add(
                    stats[:, sc:sc + 1], stats[:, c:c + 1], stats[:, c + 1:c + 2]))
                opn("vector", lambda e, c=pc, sc=accol: e.tensor_add(
                    stats[:, sc:sc + 1], stats[:, c + 2:c + 3],
                    stats[:, c + 3:c + 4]))
                opn("vector", lambda e, c=sumcol: e.tensor_scalar_mul(
                    stats[:, 2:3], stats[:, c:c + 1], 1.0 / H))
                opn("vector", lambda e, c=accol: e.tensor_scalar_mul(
                    stats[:, 3:4], stats[:, c:c + 1], 1.0 / H))
                opn("vector", lambda e: e.tensor_mul(
                    stats[:, 4:5], stats[:, 2:3], stats[:, 2:3]))
                opn("vector", lambda e: e.tensor_sub(
                    stats[:, 5:6], stats[:, 3:4], stats[:, 4:5]))
                opn("vector", lambda e: e.tensor_scalar_add(
                    stats[:, 6:7], stats[:, 5:6], 1e-5))
                opn("vector", lambda e: e.tensor_scalar(
                    stats[:, 7:8], stats[:, 6:7], -0.5, 1.5, ALU.mult, ALU.add))
                opn("vector", lambda e: e.tensor_scalar_max(
                    stats[:, 8:9], stats[:, 7:8], 0.25))
                cur, alt = 8, 12
                for _ in range(3):
                    opn("vector", lambda e, c=cur: e.tensor_mul(
                        stats[:, 9:10], stats[:, c:c + 1], stats[:, c:c + 1]))
                    opn("vector", lambda e: e.tensor_mul(
                        stats[:, 10:11], stats[:, 9:10], stats[:, 6:7]))
                    opn("vector", lambda e: e.tensor_scalar(
                        stats[:, 11:12], stats[:, 10:11], -0.5, 1.5,
                        ALU.mult, ALU.add))
                    opn("vector", lambda e, c=cur, a=alt: e.tensor_mul(
                        stats[:, a:a + 1], stats[:, c:c + 1], stats[:, 11:12]))
                    cur, alt = alt, cur
                t_xn = 0
                for hh in range(2):
                    W("vector", "t", st["tr_last_t"])
                    t_xn = op("vector", "v",
                              lambda e, xa=xtile[buf], c=cur, hh=hh:
                              e.tensor_scalar(
                                  xn_t[:, :],
                                  xa[:, hh * 1024:(hh + 1) * 1024],
                                  stats[:, 2:3], stats[:, c:c + 1],
                                  ALU.subtract, ALU.mult))
                    for gq in range(2):
                        bidx = gq % 2
                        bank_wait("tensor", bidx)
                        W("tensor", "v", t_xn)
                        for q in range(4):
                            g = gq * 4 + q
                            bld = (lambda e, g=g, q=q, bi=bidx: e.transpose(
                                pb[bi][:, q * 128:(q + 1) * 128],
                                xn_t[:, g * 128:(g + 1) * 128], ident_s[:, :]))
                            if q < 3:
                                opn("tensor", bld)
                            else:
                                t_tr = op("tensor", "t", bld)
                        st["tr_last_t"] = t_tr
                        W("vector", "t", t_tr)
                        gq8 = hh * 2 + gq
                        ev = op("vector", "v",
                                lambda e, gq8=gq8, bi=bidx, r0=r0:
                                e.tensor_copy(
                                    xnT[:, gq8 * 4:gq8 * 4 + 4, r0:r0 + 128],
                                    pb[bi].rearrange("p (i c) -> p i c", i=4)))
                        bank_guard[bidx] = ("v", ev)
                xt_guard[buf] = (t_xn, t_sq)
                xn_cnt_hist[ti % 2] = t_xn
            for bk in (6, 7):
                bank_guard[bk] = ("s", st["t_sq_last"])
            t_xnT8 = op("vector", "v", lambda e, b=b: e.tensor_copy(
                xnT8[:, :, b, :], xnT[:, :, 1:CT + 1]))
            xnT8_ready[b] = t_xnT8

        emit_prep(0)
        for b in range(B):
            proj_specs = ((tmk_s, omtk, wkb_d, wk_d, "gck"),
                          (tmv_s, omtv, wvb_d, wv_d, "gcv"),
                          (tmr_s, omtr, wrb_d, wr_d, "gcr"))
            t_prev_w_mm = wo_mm_last
            ek_done = tr_done = vt_done = 0
            for w in range(3):
                tm_sb, om_sb, W_dram, W_f32, csem = proj_specs[w]
                W("vector", "t", t_prev_w_mm)
                mix_by_g = {}
                for g in range(G):
                    opn("vector", lambda e, g=g, tm=tm_sb: e.tensor_scalar(
                        s1[0][:, :], xnT[:, g, 1:CT + 1], tm[:, g:g + 1], None,
                        ALU.mult))
                    mix_by_g[g] = op("vector", "v", lambda e, g=g, om=om_sb:
                                     e.scalar_tensor_tensor(
                        mixT[:, g, :], xnT[:, g, 0:CT], om[:, g:g + 1],
                        s1[0][:, :], ALU.mult, ALU.add))
                t_mix = mix_by_g[G - 1]

                t_base = cnt["t"]
                pred = {h: t_base + h + 1 for h in range(G)}
                wld = []
                if b == 0:
                    W("sync", csem, cast_done[csem])
                for h in range(G):
                    bufi = h % 2
                    W("sync", "t", gw_guard[bufi])
                    gws = "gw%d" % bufi
                    wld.append((gws, dma("sync", gws, wst[bufi][:, :, :],
                                         W_dram[h, :, :, :])))
                    gw_guard[bufi] = pred[h]
                for h in range(G):
                    bufi = h % 2
                    bidx = w if h % 2 == 0 else w + 4
                    W("tensor", wld[h][0], wld[h][1])
                    bank_wait("tensor", bidx)
                    for g in range(G):
                        if h == 0:
                            W("tensor", "v", mix_by_g[g])
                        bld = (lambda e, g=g, bi=bufi, bk=bidx: e.matmul(
                            pb[bk][:, :], wst[bi][:, g, :], mixT[:, g, :],
                            start=(g == 0), stop=(g == G - 1)))
                        if g < G - 1:
                            opn("tensor", bld)
                        else:
                            t_mm = op("tensor", "t", bld)
                    assert t_mm == pred[h]
                    if w == 0:
                        W("scalar", "t", t_mm)
                        ev = op("scalar", "s", lambda e, h=h, bk=bidx:
                                e.activation(ek[:, h, :], pb[bk][:, :], AF.Exp))
                        bank_guard[bidx] = ("s", ev)
                        ek_done = ev
                    elif w == 1:
                        W("vector", "t", t_mm)
                        ev = op("vector", "v", lambda e, h=h, bk=bidx:
                                e.tensor_copy(vT[:, h, :], pb[bk][:, :]))
                        bank_guard[bidx] = ("v", ev)
                        vt_done = ev
                    else:
                        W("scalar", "t", t_mm)
                        ev = op("scalar", "s", lambda e, h=h, bk=bidx:
                                e.activation(tanhr[:, h, :], pb[bk][:, :],
                                             AF.Tanh, scale=0.5))
                        bank_guard[bidx] = ("s", ev)
                        tr_done = ev
                t_prev_w_mm = t_mm
            t_proj_mm_last = t_prev_w_mm

            W("vector", "s", ek_done)
            for g in range(G):
                wb = wdec[:, g:g + 1].broadcast_to([128, CT])
                opn("vector", lambda e, g=g: e.tensor_mul(
                    ekv_s[:, :], ek[:, g, :], vT[:, g, :]))
                opn("vector", lambda e, wb=wb: e.tensor_tensor_scan(
                    atr[:, 1:CT + 1], wb, ekv_s[:, :], 0.0, ALU.mult, ALU.add))
                opn("vector", lambda e, g=g: e.tensor_copy(
                    afbf[:, g:g + 1], atr[:, CT:CT + 1]))
                opn("vector", lambda e, g=g, wb=wb: e.tensor_tensor_scan(
                    btr[:, 1:CT + 1], wb, ek[:, g, :], 0.0, ALU.mult, ALU.add))
                bld = (lambda e, g=g: e.tensor_copy(
                    afbf[:, G + g:G + g + 1], btr[:, CT:CT + 1]))
                if g < G - 1:
                    opn("vector", bld)
                else:
                    t_scan = op("vector", "v", bld)

            W("gpsimd", "v", t_scan)
            t_agin = dma("gpsimd", "gmisc", agin[b][:, :], afbf[:, :])
            W("gpsimd", "gmisc", t_agin)
            cnt["cc"] += 1
            emit("gpsimd", lambda e, b=b: e.collective_compute(
                "AllGather", ALU.bypass, replica_groups=[list(range(NCORE))],
                ins=[agin[b].ap().opt()], outs=[agout[b].ap().opt()],
            ).then_inc(sems["cc"], 1))
            t_cc = cnt["cc"]
            if b == 0:
                emit_cast("gcl", wlb_d, wl_d)
            ffn12_drain(b, max_units=40)
            if b + 1 < B:
                emit_prep(b + 1)

            W("gpsimd", "cc", t_cc)
            gre = agout[b].ap().rearrange("(m p) c -> p m c", p=128)
            dma("gpsimd", "gmisc", g_sb[:, 0:4, :], gre[:, 0:4, :])
            t_G = dma("gpsimd", "gmisc", g_sb[:, 4:8, :], gre[:, 4:8, :])

            W("vector", "gmisc", t_G)
            W("vector", "s", t_wm)
            for q in range(2):
                opn("vector", lambda e, q=q: e.tensor_mul(
                    acca[:, :], g_sb[:, 0, q * G:(q + 1) * G], wm_s[:, 0:G]))
                accs = [acca, accb]
                src = acca
                for m in range(1, 8):
                    opn("vector", lambda e, q=q, m=m: e.tensor_mul(
                        tmpm[:, :], g_sb[:, m, q * G:(q + 1) * G],
                        wm_s[:, m * G:(m + 1) * G]))
                    if m == 7:
                        opn("vector", lambda e, q=q, sc=src: e.tensor_add(
                            sab[:, q * G:(q + 1) * G], sc[:, :], tmpm[:, :]))
                    else:
                        dst = accs[m % 2]
                        opn("vector", lambda e, d=dst, sc=src: e.tensor_add(
                            d[:, :], sc[:, :], tmpm[:, :]))
                        src = dst
            W("vector", "t", t_proj_mm_last)
            t_rec_by_g = {}
            den_by_g = {-1: 0}

            def out_phase1(g):
                wb = wdec[:, g:g + 1].broadcast_to([128, CT])
                par = g % 2
                W("gpsimd", "v", den_by_g[g - 1])
                t_ekv = op("gpsimd", "g", lambda e, g=g: e.tensor_mul(
                    ekv_s[:, :], ek[:, g, :], vT[:, g, :]))
                W("vector", "g", t_ekv)
                opn("vector", lambda e, g=g, wb=wb: e.tensor_tensor_scan(
                    atr[:, 1:CT + 1], wb, ekv_s[:, :], sab[:, g:g + 1],
                    ALU.mult, ALU.add))
                opn("vector", lambda e, g=g: e.tensor_copy(
                    atr[:, 0:1], sab[:, g:g + 1]))
                opn("vector", lambda e, g=g, wb=wb: e.tensor_tensor_scan(
                    btr[:, 1:CT + 1], wb, ek[:, g, :],
                    sab[:, G + g:G + g + 1], ALU.mult, ALU.add))
                opn("vector", lambda e, g=g: e.tensor_copy(
                    btr[:, 0:1], sab[:, G + g:G + g + 1]))
                num_by_g[g] = op("vector", "v", lambda e, g=g, p=par:
                                 e.scalar_tensor_tensor(
                    s1[p][:, :], ekv_s[:, :], etf_s[:, g:g + 1], atr[:, 0:CT],
                    ALU.mult, ALU.add))
                t_den = op("vector", "v", lambda e, g=g, p=par:
                           e.scalar_tensor_tensor(
                    s2[p][:, :], ek[:, g, :], etf_s[:, g:g + 1], btr[:, 0:CT],
                    ALU.mult, ALU.add))
                den_by_g[g] = t_den
                W("scalar", "v", t_den)
                opn("scalar", lambda e, p=par: e.activation(
                    s3[p][:, :], s2[p][:, :], AF.Ln))
                t_rec_by_g[g] = op("scalar", "s", lambda e, p=par: e.activation(
                    s4[p][:, :], s3[p][:, :], AF.Exp, bias=nhb[:, 0:1],
                    scale=-1.0))

            num_by_g = {}

            rr_prev = [0]

            def out_phase2(g, last):
                par = g % 2
                W("gpsimd", "s", t_rec_by_g[g])
                W("gpsimd", "v", num_by_g[g])
                W("gpsimd", "v", rr_prev[0])
                t_wkv = op("gpsimd", "g", lambda e, p=par: e.tensor_mul(
                    s5[:, :], s1[p][:, :], s4[p][:, :]))
                W("vector", "g", t_wkv)
                t_rr = op("vector", "v", lambda e, g=g: e.scalar_tensor_tensor(
                    mixT[:, g, :], tanhr[:, g, :], 1.0, s5[:, :],
                    ALU.add, ALU.mult))
                rr_prev[0] = t_rr
                return t_rr

            rr_by_g = {}
            out_phase1(0)
            for g in range(1, G):
                out_phase1(g)
                rr_by_g[g - 1] = out_phase2(g - 1, False)
            rr_by_g[G - 1] = t_rrwkv = out_phase2(G - 1, True)
            st["t_rrwkv_prev"] = t_rrwkv

            banks = [0, 1, 4, 5]
            for ob in range(4):
                t_base = cnt["t"]
                pred = {h: t_base + h + 1 for h in range(G)}
                woldc = []
                if b == 0 and ob == 0:
                    W("sync", "gco", cast_done["gco"])
                for h in range(G):
                    bufi = h % 2
                    W("sync", "t", gw_guard[bufi])
                    gws = "gw%d" % bufi
                    woldc.append((gws, dma(
                        "sync", gws,
                        wst[bufi].ap().rearrange("p g c -> p (g c)")[:, 0:512],
                        wob_d[h, ob, :, :])))
                    gw_guard[bufi] = pred[h]
                for h in range(G):
                    bufi = h % 2
                    W("tensor", woldc[h][0], woldc[h][1])
                    W("tensor", "v", rr_by_g[h])
                    if h == 0:
                        for bk in banks:
                            bank_wait("tensor", bk)
                    for tsub in range(4):
                        bld = (lambda e, h=h, ts=tsub, bi=bufi, bk=banks[tsub]:
                               e.matmul(
                                   pb[bk][:, :],
                                   mixT[:, h, ts * 128:(ts + 1) * 128],
                                   wst[bi].ap().rearrange(
                                       "p g c -> p (g c)")[:, 0:512],
                                   start=(h == 0), stop=(h == G - 1)))
                        if tsub < 3:
                            opn("tensor", bld)
                        else:
                            t_mm = op("tensor", "t", bld)
                    assert t_mm == pred[h]
                for tsub in range(4):
                    ai = tsub % 2
                    W("vector", "t", t_mm)
                    W("vector", "sa" if ai == 0 else "sa1",
                      at_guard.get(ai, 0))
                    tev = op("vector", "v", lambda e, bk=banks[tsub], ai=ai:
                             e.tensor_copy(attw[ai][:, :], pb[bk][:, :]))
                    bank_guard[banks[tsub]] = ("v", tev)
                    W("gpsimd", "v", tev)
                    at_guard[ai] = dma("gpsimd", "sa" if ai == 0 else "sa1",
                                       attn_d[b, tsub * 128:(tsub + 1) * 128,
                                              ob * 512:(ob + 1) * 512],
                                       attw[ai][:, :])
                wo_mm_last = t_mm
            ffn12_drain(b)

        ffn12_drain(B - 1)
        attn_ctx.close()

        ffn_ctx = ExitStack()
        fa = ffn_ctx.enter_context
        wlsb = fa(nc.sbuf_tensor("wlsb", [128, NJ, 1024], F8))
        actsb = [fa(nc.sbuf_tensor("actsb%d" % i, [128, NJ, 512], F8))
                 for i in range(2)]
        f1t = [fa(nc.sbuf_tensor("f1t%d" % i, [128, 512], BF16))
               for i in range(2)]
        xrt = [fa(nc.sbuf_tensor("xrt%d" % i, [128, 512], F32))
               for i in range(2)]
        att2 = [fa(nc.sbuf_tensor("att2_%d" % i, [128, 512], BF16))
                for i in range(2)]
        o1t = [fa(nc.sbuf_tensor("o1t%d" % i, [128, 512], F32))
               for i in range(2)]
        outt = [fa(nc.sbuf_tensor("outt%d" % i, [128, 512], F32))
                for i in range(2)]

        t_act_s0, t_act_s1 = cnt["gs0"], cnt["gs1"]
        t_attn_sa = cnt["sa"]
        t_attn_sa1 = cnt["sa1"]

        t_ffn3_mm = 0
        ev_guard = {}
        ob2 = 0
        out_guard = {0: 0, 1: 0}
        xr_guard = {0: 0, 1: 0}
        W("sync", "sa", t_attn_sa)
        W("sync", "sa1", t_attn_sa1)
        W("sync", "gcl", cast_done["gcl"])
        DR = mybir.MatmulPerfMode.DoubleRow
        act_ld_cnt = {0: 0, 1: 0}
        mm_done = {}
        for obh in range(2):
            W("sync", "t", t_ffn3_mm)
            wl_ld = 0
            for j in range(NJ):
                wl_ld = dma("sync", "gwl", wlsb[:, j, :],
                            wlb_d[j * 128:(j + 1) * 128,
                                  obh * 1024:(obh + 1) * 1024])
            for bb in range(B):
                w = obh * B + bb
                par = w % 2
                if act_ld_cnt[par] == 0 or True:
                    W("gpsimd", "gs0", t_act_s0)
                    W("gpsimd", "gs1", t_act_s1)
                    W("gpsimd", "t", mm_done.get(w - 2, 0))
                    lsem = "lact" if par == 0 else "lact1"
                    for j0 in range(0, NJ, 8):
                        act_ld_cnt[par] = dma(
                            "gpsimd", lsem, actsb[par][:, j0:j0 + 8, :],
                            act_d[bb, j0 * 128:(j0 + 8) * 128, :]
                            .rearrange("(j p) t -> p j t", p=128))
                W("tensor", "gwl", wl_ld)
                if obh == 0 and bb == 0:
                    for bk in range(8):
                        bank_wait("tensor", bk)
                W("tensor", "lact" if par == 0 else "lact1", act_ld_cnt[par])
                t_half = {}
                for half in range(2):
                    tsubs = (0, 1) if half == 0 else (2, 3)
                    for ts_ in tsubs:
                        for ot in range(2):
                            W("tensor", "v", ev_guard.get(ts_ * 2 + ot, 0))
                    for jp in range(NJ // 2):
                        for ts_ in tsubs:
                            for ot in range(2):
                                bld = (lambda e, jp=jp, ts=ts_, ot=ot, p2=par:
                                       e.matmul(
                                    pb[ts * 2 + ot][:, :],
                                    actsb[p2][:, 2 * jp:2 * jp + 2,
                                              ts * 128:(ts + 1) * 128],
                                    wlsb[:, 2 * jp:2 * jp + 2,
                                         ot * 512:(ot + 1) * 512],
                                    start=(jp == 0),
                                    stop=(jp == NJ // 2 - 1),
                                    perf_mode=DR))
                                if (jp == NJ // 2 - 1 and ts_ == tsubs[-1]
                                        and ot == 1):
                                    t_half[half] = op("tensor", "t", bld)
                                else:
                                    opn("tensor", bld)
                t_ffn3_mm = t_half[1]
                mm_done[w] = t_ffn3_mm
                for tsub in range(4):
                    for ot in range(2):
                        col0 = obh * 1024 + ot * 512
                        W("vector", "t", t_half[0 if tsub < 2 else 1])
                        tfv = op("vector", "v", lambda e, bk=tsub * 2 + ot,
                                 c=ob2: e.tensor_copy(f1t[c][:, :],
                                                      pb[bk][:, :]))
                        ev_guard[tsub * 2 + ot] = tfv
                        W("sync", "v", xr_guard[ob2])
                        lfs = "lf%d" % ob2
                        dma("sync", lfs, xrt[ob2][:, :],
                            x_d[bb, 1 + tsub * 128:1 + (tsub + 1) * 128,
                                col0:col0 + 512])
                        t_al = dma("sync", lfs, att2[ob2][:, :],
                                   attn_d[bb, tsub * 128:(tsub + 1) * 128,
                                          col0:col0 + 512])
                        W("vector", lfs, t_al)
                        W("vector", "so%d" % ob2, out_guard[ob2])
                        opn("vector", lambda e, c=ob2: e.tensor_add(
                            o1t[c][:, :], xrt[c][:, :], att2[c][:, :]))
                        t_ov = op("vector", "v", lambda e, c=ob2: e.tensor_add(
                            outt[c][:, :], o1t[c][:, :], f1t[c][:, :]))
                        xr_guard[ob2] = t_ov
                        W("sync", "v", t_ov)
                        out_guard[ob2] = dma(
                            "sync", "so%d" % ob2,
                            out_d[bb, tsub * 128:(tsub + 1) * 128,
                                  col0:col0 + 512],
                            outt[ob2][:, :])
                        ob2 ^= 1

        ffn_ctx.close()

        W("sync", "so0", cnt["so0"])
        W("sync", "so1", cnt["so1"])
        W("gpsimd", "gs0", cnt["gs0"])
        W("gpsimd", "gs1", cnt["gs1"])
        W("gpsimd", "sa", cnt["sa"])
        W("gpsimd", "sa1", cnt["sa1"])

        block = ctx.enter_context(nc.Block())

        @block.sync
        def _(e):
            for f in steps["sync"]:
                f(e)

        @block.gpsimd
        def _(e):
            for f in steps["gpsimd"]:
                f(e)

        @block.vector
        def _(e):
            for f in steps["vector"]:
                f(e)

        @block.scalar
        def _(e):
            for f in steps["scalar"]:
                f(e)

        @block.tensor
        def _(e):
            for f in steps["tensor"]:
                f(e)

    return nc


_NC_CACHE = []


def kernel(L0_attn_hidden, L0_attn_time_decay, L0_attn_time_first,
           L0_attn_time_mix_key, L0_attn_time_mix_value,
           L0_attn_time_mix_receptance, L0_attn_Wk, L0_attn_Wv, L0_attn_Wr,
           L0_attn_Wo, L0_ffn_w_gating, L0_ffn_w_linear):
    x = np.ascontiguousarray(np.asarray(L0_attn_hidden, dtype=np.float32))
    x_pad = np.concatenate([np.zeros((B, 1, H), np.float32), x], axis=1)

    def vec16(v):
        return np.ascontiguousarray(
            np.asarray(v, np.float32).reshape(-1).reshape(G, 128).T)

    def tile_proj(w):
        return np.ascontiguousarray(
            np.asarray(w, np.float32).reshape(G, 128, G, 128)
            .transpose(2, 1, 0, 3))

    wk = tile_proj(L0_attn_Wk)
    wv = tile_proj(L0_attn_Wv)
    wr = tile_proj(L0_attn_Wr)
    wo = np.ascontiguousarray(
        np.asarray(L0_attn_Wo, np.float32).reshape(G, 128, 4, 512)
        .transpose(0, 2, 1, 3))
    wg = np.asarray(L0_ffn_w_gating, np.float32)

    def tile_ffn(w):
        return np.ascontiguousarray(
            w.reshape(G, 128, NJ, 128).transpose(2, 1, 0, 3))

    wg0 = tile_ffn(wg[0])
    wg1 = tile_ffn(wg[1])
    wl = np.ascontiguousarray(np.asarray(L0_ffn_w_linear, np.float32))
    tmk = vec16(L0_attn_time_mix_key)
    tmv = vec16(L0_attn_time_mix_value)
    tmr = vec16(L0_attn_time_mix_receptance)
    td = vec16(L0_attn_time_decay)
    tf = vec16(L0_attn_time_first)
    ident = np.eye(128, dtype=np.float32)

    if not _NC_CACHE:
        _NC_CACHE.append(build_program())
    nc = _NC_CACHE[0]

    in_maps = []
    for c in range(NCORE):
        row = np.array([512.0 * (c - 1 - m) if m < c else 1e9
                        for m in range(8)], np.float32)
        ecnt = np.ascontiguousarray(np.broadcast_to(row, (128, 8)))
        xs = np.ascontiguousarray(x_pad[:, c * CT:c * CT + CT + 1, :])
        in_maps.append({
            "x": xs, "wk": wk, "wv": wv, "wr": wr, "wo": wo,
            "wg0": wg0, "wg1": wg1, "wl": wl,
            "tmk": tmk, "tmv": tmv, "tmr": tmr, "td": td, "tf": tf,
            "ecnt": ecnt, "ident": ident,
        })

    res = run_bass_kernel_spmd(nc, in_maps, core_ids=list(range(NCORE)))
    out = np.empty((B, T, H), np.float32)
    for c in range(NCORE):
        out[:, c * CT:(c + 1) * CT, :] = res.results[c]["out"]
    return out
